# version 17
# baseline (speedup 1.0000x reference)
import math
import numpy as np

import concourse.bass as bass
import concourse.mybir as mybir
from concourse.bass_utils import run_bass_kernel_spmd

F32 = mybir.dt.float32
BF16 = mybir.dt.bfloat16
AF = mybir.ActivationFunctionType
ALU = mybir.AluOpType
AX = mybir.AxisListType

D = 1024
NH = 16
DH = 64
MF = 256
FH = 4096
NE = 8
EPS_LN = 1e-5
EPS_PHI = 1e-6 / 16.0
NEG_LN16 = -math.log(16.0)
C_ORF = 1.0 / (64.0 ** 0.25)
C_SQ = -1.0 / 16.0


CONSTS = [EPS_LN, NEG_LN16, -0.5, -1.0, EPS_PHI, 1.0, 0.5, -2.0]


def build(T=4096, L_total=8192, debug=False, ncores=8):
    NST = T // 512
    NSC = T // 256
    nc = bass.Bass("TRN2", target_bir_lowering=False)

    x_in = nc.dram_tensor("x", [T, D], F32, kind="ExternalInput").ap()
    wqT = nc.dram_tensor("wqT", [D, D], F32, kind="ExternalInput").ap()
    wkT = nc.dram_tensor("wkT", [D, D], F32, kind="ExternalInput").ap()
    wvT = nc.dram_tensor("wvT", [D, D], F32, kind="ExternalInput").ap()
    woT = nc.dram_tensor("woT", [D, D], F32, kind="ExternalInput").ap()
    orfT_in = nc.dram_tensor("orfT", [DH, MF], F32, kind="ExternalInput").ap()
    bq_in = nc.dram_tensor("bq", [128, 8], F32, kind="ExternalInput").ap()
    bk_in = nc.dram_tensor("bk", [128, 8], F32, kind="ExternalInput").ap()
    bv_in = nc.dram_tensor("bvfm", [128, 8], F32, kind="ExternalInput").ap()
    bo_in = nc.dram_tensor("borow", [1, D], F32, kind="ExternalInput").ap()
    gw1T = nc.dram_tensor("gw1T", [D, D], F32, kind="ExternalInput").ap()
    gb1_in = nc.dram_tensor("gb1", [128, 8], F32, kind="ExternalInput").ap()
    gw2T_in = nc.dram_tensor("gw2T", [D, NE], F32, kind="ExternalInput").ap()
    gb2_in = nc.dram_tensor("gb2row", [1, NE], F32, kind="ExternalInput").ap()
    w1_in = nc.dram_tensor("w1h", [FH, 512], F32, kind="ExternalInput").ap()
    tw1_in = nc.dram_tensor("tw1h", [32, 128, NE * 512], F32, kind="ExternalInput").ap()
    b1_in = nc.dram_tensor("b1c", [128, 32], F32, kind="ExternalInput").ap()
    tb1_in = nc.dram_tensor("tb1c", [128, NE * 32], F32, kind="ExternalInput").ap()
    w2_in = nc.dram_tensor("w2h", [D, 2048], F32, kind="ExternalInput").ap()
    tw2_in = nc.dram_tensor("tw2h", [8, 4, 128, NE * 512], F32, kind="ExternalInput").ap()
    b2_in = nc.dram_tensor("b2row", [1, D], F32, kind="ExternalInput").ap()
    tb2_in = nc.dram_tensor("tb2", [NE, D], F32, kind="ExternalInput").ap()
    ident_in = nc.dram_tensor("ident", [128, 128], F32, kind="ExternalInput").ap()

    out_d = nc.dram_tensor("out", [T, D], F32, kind="ExternalOutput").ap()

    s_xn1 = nc.dram_tensor("s_xn1", [D, T], BF16).ap()
    s_xn2 = nc.dram_tensor("s_xn2", [D, T], BF16).ap()
    s_x2 = nc.dram_tensor("s_x2", [T, D], F32).ap()
    CCA = 65 * NH * MF + NH * 65
    cc_a_in = nc.dram_tensor("cc_a_in", [1, CCA], F32)
    cc_a_out = nc.dram_tensor("cc_a_out", [1, CCA], F32)
    cc_g_in = nc.dram_tensor("cc_g_in", [1, NE], F32)
    cc_g_out = nc.dram_tensor("cc_g_out", [1, NE], F32)
    NWH = 128 * 16384
    cc_w1_in = nc.dram_tensor("cc_w1_in", [1, NWH], BF16)
    cc_w1_out = nc.dram_tensor("cc_w1_out", [2, NWH], BF16)
    cc_w2_in = nc.dram_tensor("cc_w2_in", [1, NWH], BF16)
    cc_w2_out = nc.dram_tensor("cc_w2_out", [2, NWH], BF16)
    RG = [[i, i + 1] for i in range(0, ncores, 2)]

    dbg = {}
    if debug:
        dbg["kv_acc"] = nc.dram_tensor("d_kvacc", [65, NH * MF], F32, kind="ExternalOutput").ap()
        dbg["vsum"] = nc.dram_tensor("d_vsum", [1, NH * 65], F32, kind="ExternalOutput").ap()
        dbg["xn1"] = nc.dram_tensor("d_xn1", [D, T], F32, kind="ExternalOutput").ap()
        dbg["x2"] = nc.dram_tensor("d_x2", [T, D], F32, kind="ExternalOutput").ap()
        dbg["g"] = nc.dram_tensor("d_g", [1, NE], F32, kind="ExternalOutput").ap()
        dbg["mw1t"] = nc.dram_tensor("d_mw1t", [D, FH], F32, kind="ExternalOutput").ap()

    def x_st_ap(dram, st, width=512):
        return dram[st * width:(st + 1) * width, :].rearrange("(s p) d -> p s d", p=128)

    def fm_st_ap(dram, st, width=512):
        return dram[:, st * width:(st + 1) * width].rearrange("(c p) t -> p c t", p=128)

    import contextlib
    with contextlib.ExitStack() as _es:
        idb = _es.enter_context(nc.sbuf_tensor("idb", [128, 128], BF16))
        cpool = _es.enter_context(nc.sbuf_tensor("cpool", [128, 8], F32))
        ones_row = _es.enter_context(nc.sbuf_tensor("ones_row", [1, 128], BF16))
        ones_col = _es.enter_context(nc.sbuf_tensor("ones_col", [128, 1], BF16))
        onef_col = _es.enter_context(nc.sbuf_tensor("onef_col", [128, 1], F32))
        g_row = _es.enter_context(nc.sbuf_tensor("g_row", [1, NE], F32))
        g_cols = _es.enter_context(nc.sbuf_tensor("g_cols", [128, NE], F32))
        g8 = _es.enter_context(nc.sbuf_tensor("g8", [NE, 1], BF16))
        mb1c = _es.enter_context(nc.sbuf_tensor("mb1c", [128, 32], F32))
        mb2row = _es.enter_context(nc.sbuf_tensor("mb2row", [1, D], BF16))
        ccg_sb = _es.enter_context(nc.sbuf_tensor("ccg_sb", [1, NE], F32))
        qxA0 = _es.enter_context(nc.semaphore("qxA0"))
        qxA1 = _es.enter_context(nc.semaphore("qxA1"))
        qxP = [qxA0, qxA1]
        qsp0 = _es.enter_context(nc.semaphore("qsp0"))
        qsp1 = _es.enter_context(nc.semaphore("qsp1"))
        qsP = [qsp0, qsp1]
        qwa = _es.enter_context(nc.semaphore("qwa"))
        qwg = _es.enter_context(nc.semaphore("qwg"))
        qws = _es.enter_context(nc.semaphore("qws"))
        a_sq = _es.enter_context(nc.semaphore("a_sq"))
        d_st = _es.enter_context(nc.semaphore("d_st"))
        a_inv = _es.enter_context(nc.semaphore("a_inv"))
        d_nm = _es.enter_context(nc.semaphore("d_nm"))
        a_nrm = _es.enter_context(nc.semaphore("a_nrm"))
        t_tp = _es.enter_context(nc.semaphore("t_tp"))
        a_etp = _es.enter_context(nc.semaphore("a_etp"))
        t_kp = _es.enter_context(nc.semaphore("t_kp"))
        a_ek = _es.enter_context(nc.semaphore("a_ek"))
        t_vp = _es.enter_context(nc.semaphore("t_vp"))
        a_ev = _es.enter_context(nc.semaphore("a_ev"))
        t_phi = _es.enter_context(nc.semaphore("t_phi"))
        a_ephi = _es.enter_context(nc.semaphore("a_ephi"))
        t_kvs = _es.enter_context(nc.semaphore("t_kvs"))
        d_kv = _es.enter_context(nc.semaphore("d_kv"))
        d_xs = _es.enter_context(nc.semaphore("d_xs"))
        t_vs = _es.enter_context(nc.semaphore("t_vs"))
        a_evs = _es.enter_context(nc.semaphore("a_evs"))
        g_cc = _es.enter_context(nc.semaphore("g_cc"))
        qcc = _es.enter_context(nc.semaphore("qcc"))
        d_pre = _es.enter_context(nc.semaphore("d_pre"))
        bar = _es.enter_context(nc.semaphore("bar"))
        t_bt = _es.enter_context(nc.semaphore("t_bt"))
        a_ebt = _es.enter_context(nc.semaphore("a_ebt"))
        d_ebt = _es.enter_context(nc.semaphore("d_ebt"))
        t_c2 = _es.enter_context(nc.semaphore("t_c2"))
        a_ec2 = _es.enter_context(nc.semaphore("a_ec2"))
        d_dc = _es.enter_context(nc.semaphore("d_dc"))
        t_bo = _es.enter_context(nc.semaphore("t_bo"))
        a_ebo = _es.enter_context(nc.semaphore("a_ebo"))
        onesf_row = _es.enter_context(nc.sbuf_tensor("onesf_row", [1, 128], F32))
        dss = _es.enter_context(nc.semaphore("dss"))
        ass = _es.enter_context(nc.semaphore("ass"))
        _ss = {"v": [dss, 0], "a": [ass, 0]}

        def ss_dep(key, ins):
            _ss[key][1] += 1
            ins.then_inc(_ss[key][0], 1)

        def ss_wait(eng, key):
            eng.wait_ge(_ss[key][0], _ss[key][1])
        gb2_sb = _es.enter_context(nc.sbuf_tensor("gb2_sb", [1, NE], F32))
        _names_b = ("qxB0 qxB1 qnB0 qnB1 qx2s qn2s qwb t_qp a_eq t_phq a_ephq "
                    "t_nd a_r2 d_at t_wo d_x2 a_sq2 d_st2 a_inv2 d_nm2 a_nrm2 "
                    "t_tp2 a_etp2 a_en2 t_gp a_eg d_hs t_g a_gv qcg d_g t_g2 "
                    "a_g2 g_cc2").split()
        _S = {n: _es.enter_context(nc.semaphore(n)) for n in _names_b}
        (qxB0, qxB1, qnB0, qnB1, qx2s, qn2s, qwb, t_qp, a_eq, t_phq, a_ephq,
         t_nd, a_r2, d_at, t_wo, d_x2, a_sq2, d_st2, a_inv2, d_nm2, a_nrm2,
         t_tp2, a_etp2, a_en2, t_gp, a_eg, d_hs, t_g, a_gv, qcg, d_g, t_g2,
         a_g2, g_cc2) = (_S[n] for n in _names_b)
        qxBP = [qxB0, qxB1]
        qnBP = [qnB0, qnB1]
        for i, cv in enumerate(CONSTS):
            nc.const_aps.register(F32, cv, cpool[:, i:i + 1])
        _ab = contextlib.ExitStack()
        idf = _ab.enter_context(nc.sbuf_tensor("idf", [128, 128], F32))
        orfpad = _ab.enter_context(nc.sbuf_tensor("orfpad", [128, 512], BF16))
        sqpad = _ab.enter_context(nc.sbuf_tensor("sqpad", [128, 512], BF16))
        bkq = _ab.enter_context(nc.sbuf_tensor("bkq", [128, 16], F32))
        gb1c = _ab.enter_context(nc.sbuf_tensor("gb1c", [128, 8], F32))
        bvb = _ab.enter_context(nc.sbuf_tensor("bvb", [128, 8], BF16))
        borow_b = _ab.enter_context(nc.sbuf_tensor("borow_b", [1, D], BF16))
        bop = _ab.enter_context(nc.sbuf_tensor("boprow", [1, D], BF16))
        kv_acc = _ab.enter_context(nc.sbuf_tensor("kv_acc", [65, NH * MF], F32))
        vsum_sb = _ab.enter_context(nc.sbuf_tensor("vsum_sb", [1, NH * 65], F32))
        cv_sb = _ab.enter_context(nc.sbuf_tensor("cv_sb", [1, NH * 65], F32))
        xn1s = _ab.enter_context(nc.sbuf_tensor("xn1s", [128, NST * 8], F32))
        xn1sum = _ab.enter_context(nc.sbuf_tensor("xn1sum", [128, 8], F32))
        xn1sum_b = _ab.enter_context(nc.sbuf_tensor("xn1sum_b", [128, 8], BF16))
        bufx = _ab.enter_context(nc.sbuf_tensor("bufx", [128, NH * 2 * 128], BF16))
        c2cols = _ab.enter_context(nc.sbuf_tensor("c2cols", [128, 8], F32))
        dencols = _ab.enter_context(nc.sbuf_tensor("dencols", [128, 8], F32))
        hsum = _ab.enter_context(nc.sbuf_tensor("hsum", [128, 8 * 8 + 8], F32))
        orff = _ab.enter_context(nc.sbuf_tensor("orff", [128, 512], F32))
        with (
            nc.Block() as blk,
        ):
            @blk.sync
            def _(s):
                s.dma_start(out=idf[:, :], in_=ident_in).then_inc(qws, 16)
                s.dma_start(out=bkq[:, 0:8], in_=bq_in).then_inc(qws, 16)
                s.dma_start(out=bkq[:, 8:16], in_=bk_in).then_inc(qws, 16)
                s.dma_start(out=gb1c[:, :], in_=gb1_in).then_inc(qws, 16)
                s.dma_start(out=gb2_sb[:, :], in_=gb2_in).then_inc(qws, 16)

            @blk.gpsimd
            def _(g):
                for i, cv in enumerate(CONSTS):
                    g.memset(cpool[:, i:i + 1], cv)
                g.memset(ones_row[:, :], 1.0)
                g.memset(ones_col[:, :], 1.0)
                g.memset(onef_col[:, :], 1.0)
                g.memset(onesf_row[:, :], 1.0)
                g.memset(vsum_sb[0:1, :].rearrange("p (h x) -> p h x", x=65)[:, :, 64:65], float(T))
                g.memset(orff[:, :], 0.0)
                g.memset(sqpad[:, :], 0.0)
                g.drain()
                g.memset(sqpad[0:64, 0:256], C_SQ)
                g.memset(sqpad[64:128, 256:512], C_SQ)
                g.drain()
                g.dma_start(out=idb[:, :], in_=ident_in).then_inc(qwg, 16)
                g.dma_start(out=bvb[:, :], in_=bv_in).then_inc(qwg, 16)
                g.dma_start(out=borow_b[:, :], in_=bo_in).then_inc(qwg, 16)
                g.drain()
                g.dma_start(out=orff[0:64, 0:256], in_=orfT_in).then_inc(qwg, 16)
                g.dma_start(out=orff[64:128, 256:512], in_=orfT_in).then_inc(qwg, 16)

            @blk.vector
            def _(v):
                v.wait_ge(qwg, 80)
                v.tensor_scalar_mul(orfpad[:, :], orff[:, :], C_ORF).then_inc(d_pre, 1)

        with (
            nc.sbuf_tensor("xtA", [128, 2 * 4096], F32) as xtA,
            nc.sbuf_tensor("statA", [128, 2 * 24], F32) as statA,
            nc.sbuf_tensor("xn1tm", [128, 2 * 4096], BF16) as xn1tm,
            nc.sbuf_tensor("xn1fm", [128, 2 * 4096], BF16) as xn1fm,
            nc.sbuf_tensor("wk_sb", [128, 8192], BF16) as wk_sb,
            nc.sbuf_tensor("wv_sb", [128, 8192], BF16) as wv_sb,
            nc.sbuf_tensor("kk_st", [128, 2 * 4096], BF16) as kk_st,
            nc.sbuf_tensor("ks_st", [128, 2 * 4096], BF16) as ks_st,
            nc.sbuf_tensor("vp_st", [128, 2 * 4160], BF16) as vp_st,
            nc.sbuf_tensor("phk", [128, 2 * 2048], BF16) as phk,
            nc.sbuf_tensor("sq_sink", [128, 1024], F32) as sq_sink,
            nc.psum_tensor("ps_tpA", [128, 2 * 512], F32) as ps_tpA,
            nc.psum_tensor("ps_pjA", [128, 2 * 512], F32) as ps_pjA,
            nc.psum_tensor("ps_phA", [128, 2 * 512], F32) as ps_phA,
            nc.psum_tensor("ps_kvA", [128, 2 * 512], F32) as ps_kvA,
            nc.Block() as blk,
        ):
            def pTP(i):
                return ps_tpA[:, (i % 2) * 512:(i % 2) * 512 + 512]

            def pPJ(i):
                return ps_pjA[:, (i % 2) * 512:(i % 2) * 512 + 512]

            def pPH(i):
                return ps_phA[:, (i % 2) * 512:(i % 2) * 512 + 512]

            def pKV(i):
                return ps_kvA[:, (i % 2) * 512:(i % 2) * 512 + 512]

            @blk.sync
            def _(s):
                for st in range(min(2, NST)):
                    s.dma_start(out=xtA[:, (st % 2) * 4096:(st % 2) * 4096 + 4096],
                                in_=x_st_ap(x_in, st)).then_inc(qxP[st % 2], 16)
                for st in range(NST):
                    if st + 2 < NST:
                        s.wait_ge(a_nrm, st + 1)
                        s.dma_start(out=xtA[:, (st % 2) * 4096:(st % 2) * 4096 + 4096],
                                    in_=x_st_ap(x_in, st + 2)).then_inc(qxP[st % 2], 16)
                    s.wait_ge(a_etp, st * 8 + 8)
                    s.dma_start(out=fm_st_ap(s_xn1, st),
                                in_=xn1fm[:, (st % 2) * 4096:(st % 2) * 4096 + 4096]
                                .rearrange("p (c t) -> p c t", c=8)).then_inc(qsP[st % 2], 16)

            @blk.gpsimd
            def _(g):
                g.memset(vp_st[:, :].rearrange("p (b h x) -> p b h x", x=65, h=16)[:, :, :, 64:65], 1.0)
                g.dma_start(out=wk_sb[:, :],
                            in_=wkT[:, :].rearrange("(c p) o -> p c o", p=128)).then_inc(qwa, 16)
                g.dma_start(out=wv_sb[:, :],
                            in_=wvT[:, :].rearrange("(c p) o -> p c o", p=128)).then_inc(qwa, 16)

            @blk.scalar
            def _(a):
                for st in range(NST):
                    p = st % 2
                    xt = xtA[:, p * 4096:p * 4096 + 4096]
                    stat = statA[:, p * 24:p * 24 + 24]
                    if st == 0:
                        a.wait_ge(qws, 80)
                    a.wait_ge(qxP[st % 2], 16 * (st // 2 + 1))
                    for sub in range(4):
                        if sub == 0:
                            if st > 0:
                                a.wait_ge(a_sq, st)
                        else:
                            ss_wait(a, "a")
                        ins = a.activation(
                            sq_sink[:, :],
                            xt[:, sub * 1024:sub * 1024 + 1024],
                            AF.Square, accum_out=stat[:, 4 + sub:5 + sub])
                        if sub == 3:
                            ins.then_inc(a_sq, 1)
                        else:
                            ss_dep("a", ins)
                    a.wait_ge(d_st, st + 1)
                    ss_dep("a", a.activation(stat[:, 12:16], stat[:, 8:12], AF.Ln, bias=EPS_LN))
                    ss_wait(a, "a")
                    a.activation(stat[:, 16:20], stat[:, 12:16], AF.Exp, scale=-0.5).then_inc(a_inv, 1)
                    a.wait_ge(d_nm, st + 1)
                    if st >= 2:
                        a.wait_ge(t_tp, (st - 2) * 8 + 8)
                    for sub in range(4):
                        ins = a.activation(
                            xn1tm[:, p * 4096 + sub * 1024: p * 4096 + sub * 1024 + 1024],
                            xt[:, sub * 1024:sub * 1024 + 1024],
                            AF.Identity, bias=stat[:, 20 + sub:21 + sub],
                            scale=stat[:, 16 + sub:17 + sub])
                        if sub == 3:
                            ins.then_inc(a_nrm, 1)
                    for kc in range(8):
                        gi = st * 8 + kc
                        a.wait_ge(t_tp, gi + 1)
                        if st >= 2:
                            a.wait_ge(qsP[st % 2], 16 * ((st - 2) // 2 + 1))
                            a.wait_ge(t_vp, (st - 2) * 8 + 8)
                        ins = a.activation(
                            xn1fm[:, p * 4096 + kc * 512: p * 4096 + kc * 512 + 512],
                            pTP(gi), AF.Copy,
                            accum_out=xn1s[:, st * 8 + kc:st * 8 + kc + 1])
                        ins.then_inc(a_etp, 1)
                    for oc in range(8):
                        gi = st * 8 + oc
                        a.wait_ge(t_kp, gi + 1)
                        if st >= 2:
                            a.wait_ge(t_phi, (st - 2) * 32 + 32)
                        a.activation(kk_st[:, p * 4096 + oc * 512: p * 4096 + oc * 512 + 512],
                                     pPJ(gi), AF.Identity, bias=bkq[:, 8 + oc:9 + oc])
                        ins = a.activation(ks_st[:, p * 4096 + oc * 512: p * 4096 + oc * 512 + 512],
                                           pPJ(gi), AF.Square, bias=bkq[:, 8 + oc:9 + oc])
                        ins.then_inc(a_ek, 1)
                    for j in range(8):
                        sub, half = j // 2, j % 2
                        gi = st * 8 + j
                        a.wait_ge(t_vp, gi + 1)
                        if st >= 2:
                            a.wait_ge(t_kvs, (st - 2) * 8 + 8)
                        dst = vp_st[:, p * 4160 + sub * 1040 + half * 520:
                                    p * 4160 + sub * 1040 + half * 520 + 520] \
                            .rearrange("p (h x) -> p h x", x=65)[:, :, 0:64]
                        ins = a.activation(dst, pPJ(8 + gi), AF.Copy)
                        ins.then_inc(a_ev, 1)
                    for oc in range(8):
                        for sub in range(4):
                            fi = st * 32 + oc * 4 + sub
                            a.wait_ge(t_phi, fi + 1)
                            if sub == 0 and st * 8 + oc >= 2:
                                a.wait_ge(t_kvs, st * 8 + oc - 1)
                            ins = a.activation(
                                phk[:, (oc % 2) * 2048 + sub * 512:
                                    (oc % 2) * 2048 + sub * 512 + 512],
                                pPH(fi), AF.Exp, bias=NEG_LN16)
                            ins.then_inc(a_ephi, 1)
                a.wait_ge(t_vs, 1)
                for half in range(2):
                    dst = vsum_sb[0:1, half * 520:half * 520 + 520] \
                        .rearrange("p (h x) -> p h x", x=65)[:, :, 0:64]
                    ins = a.activation(dst, ps_pjA[0:1, half * 512:half * 512 + 512], AF.Copy)
                ins.then_inc(a_evs, 1)

            @blk.vector
            def _(v):
                for st in range(NST):
                    p = st % 2
                    xt = xtA[:, p * 4096:p * 4096 + 4096]
                    stat = statA[:, p * 24:p * 24 + 24]
                    v.wait_ge(qxP[st % 2], 16 * (st // 2 + 1))
                    for sub in range(4):
                        ss_dep("v", v.reduce_sum(stat[:, sub:sub + 1],
                                                 xt[:, sub * 1024:sub * 1024 + 1024], AX.X))
                    ss_wait(v, "v")
                    ss_dep("v", v.tensor_scalar_mul(stat[:, 0:4], stat[:, 0:4], 1.0 / D))
                    v.wait_ge(a_sq, st + 1)
                    ss_wait(v, "v")
                    ss_dep("v", v.tensor_tensor(stat[:, 12:16], stat[:, 0:4], stat[:, 0:4], ALU.mult))
                    ss_wait(v, "v")
                    v.scalar_tensor_tensor(stat[:, 8:12], stat[:, 4:8], 1.0 / D,
                                           stat[:, 12:16], ALU.mult, ALU.subtract).then_inc(d_st, 1)
                    v.wait_ge(a_inv, st + 1)
                    v.scalar_tensor_tensor(stat[:, 20:24], stat[:, 0:4], -1.0,
                                           stat[:, 16:20], ALU.mult, ALU.mult).then_inc(d_nm, 1)
                    for oc in range(8):
                        if st > 0:
                            v.wait_ge(d_kv, (st - 1) * 8 + oc + 1)
                        v.wait_ge(t_kvs, st * 8 + oc + 1)
                        dst = kv_acc[:, oc * 2 * MF:(oc + 1) * 2 * MF]
                        if st == 0:
                            ins = v.tensor_copy(dst, ps_kvA[0:65, (oc % 2) * 512:(oc % 2) * 512 + 512])
                        else:
                            ins = v.tensor_tensor(dst, dst,
                                                  ps_kvA[0:65, (oc % 2) * 512:(oc % 2) * 512 + 512],
                                                  ALU.add)
                        ins.then_inc(d_kv, 1)
                v.wait_ge(a_etp, NST * 8)
                for kc in range(8):
                    ss_dep("v", v.reduce_sum(xn1sum[:, kc:kc + 1], xn1s[:, kc:NST * 8:8], AX.X))
                ss_wait(v, "v")
                v.tensor_copy(xn1sum_b[:, :], xn1sum[:, :]).then_inc(d_xs, 1)

            @blk.tensor
            def _(t):
                t.wait_ge(qwa, 32)
                t.wait_ge(qwg, 80)
                t.wait_ge(d_pre, 1)
                for st in range(NST):
                    p = st % 2
                    t.wait_ge(a_nrm, st + 1)
                    for kc in range(8):
                        gi = st * 8 + kc
                        if gi >= 2:
                            t.wait_ge(a_etp, gi - 1)
                        for sub in range(4):
                            ins = t.matmul(
                                pTP(gi)[:, sub * 128:sub * 128 + 128],
                                xn1tm[:, p * 4096 + sub * 1024 + kc * 128:
                                      p * 4096 + sub * 1024 + kc * 128 + 128],
                                idb[:, :], start=True, stop=True)
                        ins.then_inc(t_tp, 1)
                    t.wait_ge(a_etp, st * 8 + 8)
                    for oc in range(8):
                        gi = st * 8 + oc
                        if oc >= 2:
                            t.wait_ge(a_ek, gi - 1)
                        elif st > 0:
                            t.wait_ge(a_ev, (st - 1) * 8 + 7 + oc)
                        for kc in range(8):
                            ins = t.matmul(pPJ(gi),
                                           wk_sb[:, kc * 1024 + oc * 128: kc * 1024 + oc * 128 + 128],
                                           xn1fm[:, p * 4096 + kc * 512: p * 4096 + kc * 512 + 512],
                                           start=(kc == 0), stop=(kc == 7))
                        ins.then_inc(t_kp, 1)
                    for j in range(8):
                        sub, half = j // 2, j % 2
                        gi = st * 8 + j
                        if j >= 2:
                            t.wait_ge(a_ev, gi - 1)
                        else:
                            t.wait_ge(a_ek, st * 8 + 7 + j)
                        for kc in range(8):
                            ins = t.matmul(pPJ(8 + gi),
                                           xn1fm[:, p * 4096 + kc * 512 + sub * 128:
                                                 p * 4096 + kc * 512 + sub * 128 + 128],
                                           wv_sb[:, kc * 1024 + half * 512: kc * 1024 + half * 512 + 512],
                                           start=(kc == 0), stop=(kc == 7))
                        ins.then_inc(t_vp, 1)
                    def emit_phi(oc, st=st, p=p):
                        for sub in range(4):
                            fi = st * 32 + oc * 4 + sub
                            t.wait_ge(a_ek, st * 8 + oc + 1)
                            if fi >= 2:
                                t.wait_ge(a_ephi, fi - 1)
                            t.matmul(pPH(fi),
                                     kk_st[:, p * 4096 + oc * 512 + sub * 128:
                                           p * 4096 + oc * 512 + sub * 128 + 128],
                                     orfpad[:, :], start=True, stop=False)
                            ins = t.matmul(pPH(fi),
                                           ks_st[:, p * 4096 + oc * 512 + sub * 128:
                                                 p * 4096 + oc * 512 + sub * 128 + 128],
                                           sqpad[:, :], start=False, stop=True)
                            ins.then_inc(t_phi, 1)

                    def emit_kv(oc, st=st, p=p):
                        t.wait_ge(a_ephi, st * 32 + oc * 4 + 4)
                        t.wait_ge(a_ev, st * 8 + 8)
                        if st * 8 + oc >= 2:
                            t.wait_ge(d_kv, st * 8 + oc - 1)
                        for hh in range(2):
                            for sub in range(4):
                                ins = t.matmul(
                                    pKV(oc)[0:65, hh * 256:hh * 256 + 256],
                                    vp_st[:, p * 4160 + sub * 1040 + (2 * oc + hh) * 65:
                                          p * 4160 + sub * 1040 + (2 * oc + hh) * 65 + 65],
                                    phk[:, (oc % 2) * 2048 + sub * 512 + hh * 256:
                                        (oc % 2) * 2048 + sub * 512 + hh * 256 + 256],
                                    start=(sub == 0), stop=(sub == 3))
                        ins.then_inc(t_kvs, 1)

                    for oc in range(9):
                        if oc < 8:
                            emit_phi(oc)
                        if oc >= 1:
                            emit_kv(oc - 1)
                t.wait_ge(d_xs, 1)
                t.wait_ge(a_ev, NST * 8)
                for half in range(2):
                    for kc in range(8):
                        ins = t.matmul(ps_pjA[0:1, half * 512:half * 512 + 512],
                                       xn1sum_b[:, kc:kc + 1],
                                       wv_sb[:, kc * 1024 + half * 512: kc * 1024 + half * 512 + 512],
                                       start=(kc == 0), stop=(kc == 7))
                ins.then_inc(t_vs, 1)


        with contextlib.ExitStack() as _eb:
            xtB = _eb.enter_context(nc.sbuf_tensor("xtB", [128, 2 * 4096], F32))
            statB = _eb.enter_context(nc.sbuf_tensor("statB", [128, 2 * 24], F32))
            xn1fB = _eb.enter_context(nc.sbuf_tensor("xn1fB", [128, 2 * 4096], BF16))
            wq_sb = _eb.enter_context(nc.sbuf_tensor("wq_sb", [128, 8192], BF16))
            wo_sb = _eb.enter_context(nc.sbuf_tensor("wo_sb", [128, 8192], BF16))
            gw1_sb = _eb.enter_context(nc.sbuf_tensor("gw1_sb", [128, 8192], BF16))
            gw2_sb = _eb.enter_context(nc.sbuf_tensor("gw2_sb", [128, 64], BF16))
            qq_sl = _eb.enter_context(nc.sbuf_tensor("qq_sl", [128, 2 * 512], BF16))
            qs_sl = _eb.enter_context(nc.sbuf_tensor("qs_sl", [128, 2 * 512], BF16))
            phq = _eb.enter_context(nc.sbuf_tensor("phq", [128, 2 * 2048], BF16))
            attn_st = _eb.enter_context(nc.sbuf_tensor("attn_st", [128, 4096], BF16))
            x2st = _eb.enter_context(nc.sbuf_tensor("x2st", [128, 4096], F32))
            xn2tm = _eb.enter_context(nc.sbuf_tensor("xn2tm", [128, 4096], BF16))
            xn2fm = _eb.enter_context(nc.sbuf_tensor("xn2fm", [128, 4096], BF16))
            r2_sb = _eb.enter_context(nc.sbuf_tensor("r2_sb", [128, 2 * 512], F32))
            sinkB = _eb.enter_context(nc.sbuf_tensor("sinkB", [128, 2 * 512], BF16))
            sqsB = _eb.enter_context(nc.sbuf_tensor("sqsB", [128, 1024], F32))
            hsumb = _eb.enter_context(nc.sbuf_tensor("hsumb", [128, 8], BF16))
            ps_qpB = _eb.enter_context(nc.psum_tensor("ps_qpB", [128, 2 * 512], F32))
            ps_phB = _eb.enter_context(nc.psum_tensor("ps_phB", [128, 2 * 512], F32))
            ps_nmB = _eb.enter_context(nc.psum_tensor("ps_nmB", [128, 2 * 512], F32))
            ps_dnB = _eb.enter_context(nc.psum_tensor("ps_dnB", [128, 2 * 512], F32))
            blk = _eb.enter_context(nc.Block())
            def pQP(i):
                return ps_qpB[:, (i % 2) * 512:(i % 2) * 512 + 512]

            def pPH2(i):
                return ps_phB[:, (i % 2) * 512:(i % 2) * 512 + 512]

            def pNM(i):
                return ps_nmB[:, (i % 2) * 512:(i % 2) * 512 + 512]

            def pDN(i):
                return ps_dnB[:, (i % 2) * 512:(i % 2) * 512 + 512]

            NSPA0 = 16 * ((NST + 1) // 2)
            NSPA1 = 16 * (NST // 2)

            @blk.sync
            def _(s):
                if debug:
                    s.wait_ge(qcc, 64)
                    s.dma_start(out=dbg["kv_acc"], in_=kv_acc[:, :]).then_inc(qws, 16)
                    s.dma_start(out=dbg["vsum"], in_=vsum_sb[:, :]).then_inc(qws, 16)
                s.wait_ge(qsP[0], NSPA0)
                if NST > 1:
                    s.wait_ge(qsP[1], NSPA1)
                for st in range(min(2, NST)):
                    s.dma_start(out=xn1fB[:, (st % 2) * 4096:(st % 2) * 4096 + 4096]
                                .rearrange("p (c t) -> p c t", c=8),
                                in_=fm_st_ap(s_xn1, st)).then_inc(qnBP[st % 2], 16)
                    s.dma_start(out=xtB[:, (st % 2) * 4096:(st % 2) * 4096 + 4096],
                                in_=x_st_ap(x_in, st)).then_inc(qxBP[st % 2], 16)
                for st in range(NST):
                    p = st % 2
                    if st + 2 < NST:
                        s.wait_ge(t_qp, st * 8 + 8)
                        s.dma_start(out=xn1fB[:, p * 4096:p * 4096 + 4096]
                                    .rearrange("p (c t) -> p c t", c=8),
                                    in_=fm_st_ap(s_xn1, st + 2)).then_inc(qnBP[p], 16)
                        s.wait_ge(d_x2, st * 8 + 8)
                        s.dma_start(out=xtB[:, p * 4096:p * 4096 + 4096],
                                    in_=x_st_ap(x_in, st + 2)).then_inc(qxBP[p], 16)
                    s.wait_ge(d_x2, st * 8 + 8)
                    s.dma_start(out=x_st_ap(s_x2, st), in_=x2st[:, :]
                                .rearrange("p (s d) -> p s d", s=4)).then_inc(qx2s, 16)
                    s.wait_ge(a_en2, st * 8 + 8)
                    s.dma_start(out=fm_st_ap(s_xn2, st), in_=xn2fm[:, :]
                                .rearrange("p (c t) -> p c t", c=8)).then_inc(qn2s, 16)

            @blk.gpsimd
            def _(g):
                g.wait_ge(d_kv, NST * 8)
                g.wait_ge(a_evs, 1)
                g.wait_ge(t_vs, 1)
                g.wait_ge(d_xs, 1)
                g.wait_ge(qsP[0], NSPA0)
                if NST > 1:
                    g.wait_ge(qsP[1], NSPA1)
                g.dma_start(out=cc_a_in[0:1, 0:65 * NH * MF].rearrange("x (p m) -> (x p) m", p=65),
                            in_=kv_acc[:, :]).then_inc(qcc, 16)
                g.dma_start(out=cc_a_in[0:1, 65 * NH * MF:CCA], in_=vsum_sb[:, :]).then_inc(qcc, 16)
                g.dma_start(out=wq_sb[:, :],
                            in_=wqT[:, :].rearrange("(c p) o -> p c o", p=128)).then_inc(qwb, 16)
                g.dma_start(out=wo_sb[:, :],
                            in_=woT[:, :].rearrange("(c p) o -> p c o", p=128)).then_inc(qwb, 16)
                g.dma_start(out=gw1_sb[:, :],
                            in_=gw1T[:, :].rearrange("(c p) o -> p c o", p=128)).then_inc(qwb, 16)
                g.dma_start(out=gw2_sb[:, :],
                            in_=gw2T_in[:, :].rearrange("(c p) e -> p c e", p=128)).then_inc(qwb, 16)
                g.wait_ge(qcc, 32)
                g.collective_compute(
                    "AllReduce", ALU.add, replica_groups=RG,
                    ins=[cc_a_in[:, :]], outs=[cc_a_out[:, :]],
                ).then_inc(g_cc, 1)
                g.wait_ge(g_cc, 1)
                g.dma_start(out=kv_acc[:, :],
                            in_=cc_a_out[0:1, 0:65 * NH * MF].rearrange("x (p m) -> (x p) m", p=65)).then_inc(qcc, 16)
                g.dma_start(out=vsum_sb[:, :], in_=cc_a_out[0:1, 65 * NH * MF:CCA]).then_inc(qcc, 16)
                g.wait_ge(a_gv, 1)
                g.dma_start(out=cc_g_in[:, :], in_=ccg_sb[:, :]).then_inc(qcg, 16)
                g.wait_ge(qcg, 16)
                g.collective_compute(
                    "AllReduce", ALU.add, replica_groups=RG,
                    ins=[cc_g_in[:, :]], outs=[cc_g_out[:, :]],
                ).then_inc(g_cc2, 1)
                g.wait_ge(g_cc2, 1)
                g.dma_start(out=ccg_sb[:, :], in_=cc_g_out[:, :]).then_inc(qcg, 16)

            @blk.vector
            def _(v):
                v.wait_ge(qcc, 64)
                v.tensor_scalar_mul(cv_sb[:, :], vsum_sb[:, :], EPS_PHI).then_inc(d_dc, 1)
                for h in range(NH):
                    for mc in range(2):
                        i = h * 2 + mc
                        v.wait_ge(a_ebt, i + 1)
                        ins = v.tensor_copy(
                            bufx[:, i * 128 + 64:i * 128 + 128],
                            pDN(i)[:, 64:65].broadcast_to((128, 64)))
                        ins.then_inc(d_ebt, 1)
                for st in range(NST):
                    p = st % 2
                    for oc in range(8):
                        u = st * 8 + oc
                        v.wait_ge(t_nd, u + 1)
                        v.wait_ge(a_r2, u + 1)
                        if st > 0 and oc == 0:
                            v.wait_ge(t_wo, st * 8)
                        ins = v.scalar_tensor_tensor(attn_st[:, oc * 512:oc * 512 + 512],
                                                     pNM(u), c2cols[:, oc:oc + 1],
                                                     r2_sb[:, (u % 2) * 512:(u % 2) * 512 + 512],
                                                     ALU.add, ALU.mult)
                        ins.then_inc(d_at, 1)
                    if st > 0:
                        v.wait_ge(d_st2, st)
                    for j in range(8):
                        sub, half = j // 2, j % 2
                        u = st * 8 + j
                        v.wait_ge(t_wo, u + 1)
                        v.wait_ge(qxBP[p], 16 * (st // 2 + 1))
                        if st > 0 and j == 0:
                            v.wait_ge(qx2s, 16 * st)
                            v.wait_ge(a_nrm2, st)
                        ins = v.tensor_tensor(x2st[:, sub * 1024 + half * 512:
                                                   sub * 1024 + half * 512 + 512],
                                              xtB[:, p * 4096 + sub * 1024 + half * 512:
                                                  p * 4096 + sub * 1024 + half * 512 + 512],
                                              pQP(u), ALU.add)
                        ins.then_inc(d_x2, 1)
                    stat = statB[:, p * 24:p * 24 + 24]
                    v.wait_ge(d_x2, st * 8 + 8)
                    for sub in range(4):
                        ss_dep("v", v.reduce_sum(stat[:, sub:sub + 1],
                                                 x2st[:, sub * 1024:sub * 1024 + 1024], AX.X))
                    ss_wait(v, "v")
                    ss_dep("v", v.tensor_scalar_mul(stat[:, 0:4], stat[:, 0:4], 1.0 / D))
                    v.wait_ge(a_sq2, st + 1)
                    ss_wait(v, "v")
                    ss_dep("v", v.tensor_tensor(stat[:, 12:16], stat[:, 0:4], stat[:, 0:4], ALU.mult))
                    ss_wait(v, "v")
                    v.scalar_tensor_tensor(stat[:, 8:12], stat[:, 4:8], 1.0 / D,
                                           stat[:, 12:16], ALU.mult, ALU.subtract).then_inc(d_st2, 1)
                    v.wait_ge(a_inv2, st + 1)
                    v.scalar_tensor_tensor(stat[:, 20:24], stat[:, 0:4], -1.0,
                                           stat[:, 16:20], ALU.mult, ALU.mult).then_inc(d_nm2, 1)
                v.wait_ge(a_eg, NST * 8)
                for oc in range(8):
                    ss_dep("v", v.reduce_sum(hsum[:, 64 + oc:65 + oc],
                                             hsum[:, oc * NST:(oc + 1) * NST], AX.X))
                ss_wait(v, "v")
                v.tensor_copy(hsumb[:, :], hsum[:, 64:72]).then_inc(d_hs, 1)
                v.wait_ge(qcg, 32)
                v.scalar_tensor_tensor(g_row[:, :], ccg_sb[:, :], 1.0 / L_total,
                                       gb2_sb[:, :], ALU.mult, ALU.add).then_inc(d_g, 1)

            @blk.scalar
            def _(a):
                def a_emit_buf_c2():
                    for h in range(NH):
                        for mc in range(2):
                            i = h * 2 + mc
                            a.wait_ge(t_bt, i + 1)
                            ins = a.activation(bufx[:, i * 128:i * 128 + 64],
                                               pDN(i)[:, 0:64], AF.Copy)
                            ins.then_inc(a_ebt, 1)
                    for pr in range(8):
                        a.wait_ge(t_c2, pr + 1)
                        a.activation(c2cols[:, pr:pr + 1], ps_qpB[:, 0:1],
                                     AF.Identity, scale=EPS_PHI)
                        ins = a.activation(dencols[:, pr:pr + 1], ps_qpB[:, 1:2],
                                           AF.Identity, scale=EPS_PHI)
                        ins.then_inc(a_ec2, 1)
                a.wait_ge(t_bo, 2)
                a.activation(bop[0:1, 0:512], ps_nmB[0:1, 0:512], AF.Copy)
                a.activation(bop[0:1, 512:1024], ps_nmB[0:1, 512:1024], AF.Copy).then_inc(a_ebo, 1)
                for st in range(NST):
                    p = st % 2
                    stat = statB[:, p * 24:p * 24 + 24]
                    def a_emit_qevac(oc, st=st, p=p):
                        u = st * 8 + oc
                        a.wait_ge(t_qp, u + 1)
                        if u >= 2:
                            a.wait_ge(t_phq, (u - 2) * 4 + 4)
                        a.activation(qq_sl[:, (oc % 2) * 512:(oc % 2) * 512 + 512],
                                     pQP(u), AF.Identity, bias=bkq[:, oc:oc + 1])
                        ins = a.activation(qs_sl[:, (oc % 2) * 512:(oc % 2) * 512 + 512],
                                           pQP(u), AF.Square, bias=bkq[:, oc:oc + 1])
                        ins.then_inc(a_eq, 1)

                    def a_emit_phiq(oc, st=st, p=p):
                        for z in range(4):
                            u4 = (st * 8 + oc) * 4 + z
                            a.wait_ge(t_phq, u4 + 1)
                            if st * 8 + oc >= 2 and z == 0:
                                a.wait_ge(t_nd, st * 8 + oc - 1)
                            ins = a.activation(
                                phq[:, (oc % 2) * 2048 + z * 512:(oc % 2) * 2048 + z * 512 + 512],
                                pPH2(u4), AF.Exp, bias=NEG_LN16)
                            ins.then_inc(a_ephq, 1)

                    def a_emit_r2(oc, st=st, p=p):
                        u = st * 8 + oc
                        a.wait_ge(t_nd, u + 1)
                        ss_dep("a", a.activation(r2_sb[:, (u % 2) * 512:(u % 2) * 512 + 512],
                                                 pDN(16 + u)[:, :], AF.Ln,
                                                 bias=dencols[:, oc:oc + 1]))
                        ss_wait(a, "a")
                        ins = a.activation(r2_sb[:, (u % 2) * 512:(u % 2) * 512 + 512],
                                           r2_sb[:, (u % 2) * 512:(u % 2) * 512 + 512],
                                           AF.Exp, scale=-1.0)
                        ins.then_inc(a_r2, 1)

                    if st == 0:
                        a_emit_buf_c2()
                    for stage in range(10):
                        if stage < 8:
                            a_emit_qevac(stage)
                        if 1 <= stage <= 8:
                            a_emit_phiq(stage - 1)
                        if stage >= 2:
                            a_emit_r2(stage - 2)

                    def a_emit_xn2gate(pst):
                        for kc in range(8):
                            gi = pst * 8 + kc
                            a.wait_ge(t_tp2, gi + 1)
                            if pst >= 1 and kc == 0:
                                a.wait_ge(t_gp, pst * 8)
                                a.wait_ge(qn2s, 16 * pst)
                            ins = a.activation(xn2fm[:, kc * 512:kc * 512 + 512],
                                               pDN(32 + gi), AF.Copy)
                            ins.then_inc(a_en2, 1)
                        for oc in range(8):
                            gi = pst * 8 + oc
                            a.wait_ge(t_gp, gi + 1)
                            if gi >= 2:
                                a.wait_ge(a_eg, gi - 1)
                            ins = a.activation(sinkB[:, (oc % 2) * 512:(oc % 2) * 512 + 512],
                                               pPH2(32 + gi), AF.Relu, bias=gb1c[:, oc:oc + 1],
                                               accum_out=hsum[:, oc * NST + pst:oc * NST + pst + 1])
                            ins.then_inc(a_eg, 1)

                    if st > 0:
                        a_emit_xn2gate(st - 1)

                    a.wait_ge(d_x2, st * 8 + 8)
                    stat = statB[:, p * 24:p * 24 + 24]
                    for sub in range(4):
                        if sub == 0:
                            a.wait_ge(a_sq2, st)
                        else:
                            ss_wait(a, "a")
                        ins = a.activation(sqsB[:, :], x2st[:, sub * 1024:sub * 1024 + 1024],
                                           AF.Square, accum_out=stat[:, 4 + sub:5 + sub])
                        if sub == 3:
                            ins.then_inc(a_sq2, 1)
                        else:
                            ss_dep("a", ins)
                    a.wait_ge(d_st2, st + 1)
                    ss_dep("a", a.activation(stat[:, 12:16], stat[:, 8:12], AF.Ln, bias=EPS_LN))
                    ss_wait(a, "a")
                    a.activation(stat[:, 16:20], stat[:, 12:16], AF.Exp, scale=-0.5).then_inc(a_inv2, 1)
                    a.wait_ge(d_nm2, st + 1)
                    if st >= 1:
                        a.wait_ge(t_tp2, st * 8)
                    for sub in range(4):
                        ins = a.activation(
                            xn2tm[:, sub * 1024:sub * 1024 + 1024],
                            x2st[:, sub * 1024:sub * 1024 + 1024],
                            AF.Identity, bias=stat[:, 20 + sub:21 + sub],
                            scale=stat[:, 16 + sub:17 + sub])
                        if sub == 3:
                            ins.then_inc(a_nrm2, 1)
                    if st == NST - 1:
                        a_emit_xn2gate(st)
                a.wait_ge(t_g, 1)
                a.activation(ccg_sb[:, :], ps_nmB[0:1, 0:NE], AF.Copy).then_inc(a_gv, 1)
                a.wait_ge(t_g2, 2)
                a.activation(g8[:, :], ps_nmB[0:NE, 0:1], AF.Copy)
                ins = a.activation(g_cols[:, :], ps_dnB[:, 0:NE], AF.Copy)
                ins.then_inc(a_g2, 1)

            @blk.tensor
            def _(t):
                t.wait_ge(qwb, 64)
                for half in range(2):
                    for kc in range(8):
                        t.matmul(ps_nmB[0:1, half * 512:half * 512 + 512],
                                 bvb[:, kc:kc + 1],
                                 wo_sb[:, kc * 1024 + half * 512:kc * 1024 + half * 512 + 512],
                                 start=(kc == 0), stop=False)
                    ins = t.matmul(ps_nmB[0:1, half * 512:half * 512 + 512],
                                   ones_row[0:1, 0:1], borow_b[0:1, half * 512:half * 512 + 512],
                                   start=False, stop=True)
                    ins.then_inc(t_bo, 1)
                def emit_buf_c2():
                    t.wait_ge(d_dc, 1)
                    for h in range(NH):
                        for mc in range(2):
                            i = h * 2 + mc
                            if i >= 2:
                                t.wait_ge(a_ebt, i - 1)
                                t.wait_ge(d_ebt, i - 1)
                            t.matmul(pDN(i)[:, 0:65],
                                     kv_acc[:, h * MF + mc * 128:h * MF + mc * 128 + 128],
                                     idf[0:65, 0:65], start=True, stop=False)
                            ins = t.matmul(pDN(i)[:, 0:65],
                                           onesf_row[:, :], cv_sb[0:1, h * 65:(h + 1) * 65],
                                           start=False, stop=True)
                            ins.then_inc(t_bt, 1)
                    for pr in range(8):
                        hA, hB = 2 * pr, 2 * pr + 1
                        t.wait_ge(a_ebt, (hB + 1) * 2)
                        t.wait_ge(d_ebt, (hB + 1) * 2)
                        if pr >= 1:
                            t.wait_ge(a_ec2, pr)
                        for hh, h in ((0, hA), (1, hB)):
                            tp = (0, 0) if hh == 0 else (0, 64)
                            prr = slice(0, 64) if hh == 0 else slice(64, 128)
                            t.matmul(ps_qpB[prr, 0:1], bufx[:, (h * 2) * 128:(h * 2) * 128 + 64],
                                     ones_col[:, :], start=True, stop=False, tile_position=tp)
                            t.matmul(ps_qpB[prr, 0:1], bufx[:, (h * 2 + 1) * 128:(h * 2 + 1) * 128 + 64],
                                     ones_col[:, :], start=False, stop=True, tile_position=tp)
                            t.matmul(ps_qpB[prr, 1:2], bufx[:, (h * 2) * 128 + 64:(h * 2) * 128 + 128],
                                     ones_col[:, :], start=True, stop=False, tile_position=tp)
                            ins = t.matmul(ps_qpB[prr, 1:2],
                                           bufx[:, (h * 2 + 1) * 128 + 64:(h * 2 + 1) * 128 + 128],
                                           ones_col[:, :], start=False, stop=True, tile_position=tp)
                        ins.then_inc(t_c2, 1)
                for st in range(NST):
                    p = st % 2
                    t.wait_ge(qnBP[p], 16 * (st // 2 + 1))
                    def emit_qproj(oc, st=st, p=p):
                        u = st * 8 + oc
                        if u == 0:
                            t.wait_ge(a_ec2, 8)
                        if u >= 2:
                            t.wait_ge(a_eq, u - 1)
                        if st > 0 and oc < 2:
                            t.wait_ge(d_x2, (st - 1) * 8 + 7 + oc)
                        for kc in range(8):
                            ins = t.matmul(pQP(u),
                                           wq_sb[:, kc * 1024 + oc * 128:kc * 1024 + oc * 128 + 128],
                                           xn1fB[:, p * 4096 + kc * 512:p * 4096 + kc * 512 + 512],
                                           start=(kc == 0), stop=(kc == 7))
                        ins.then_inc(t_qp, 1)

                    def emit_phiq(oc, st=st, p=p):
                        u = st * 8 + oc
                        t.wait_ge(a_eq, u + 1)
                        for z in range(4):
                            hh, mc = z // 2, z % 2
                            u4 = u * 4 + z
                            if st > 1 and oc == 0 and z < 2:
                                t.wait_ge(a_eg, (st - 1) * 8)
                            if u4 >= 2:
                                t.wait_ge(a_ephq, u4 - 1)
                            t.matmul(pPH2(u4),
                                     orfpad[:, hh * 256 + mc * 128:hh * 256 + mc * 128 + 128],
                                     qq_sl[:, (oc % 2) * 512:(oc % 2) * 512 + 512],
                                     start=True, stop=False)
                            ins = t.matmul(pPH2(u4),
                                           sqpad[:, hh * 256 + mc * 128:hh * 256 + mc * 128 + 128],
                                           qs_sl[:, (oc % 2) * 512:(oc % 2) * 512 + 512],
                                           start=False, stop=True)
                            ins.then_inc(t_phq, 1)

                    def emit_numden(oc, st=st, p=p):
                        u = st * 8 + oc
                        t.wait_ge(a_ephq, u * 4 + 4)
                        if u >= 2:
                            t.wait_ge(d_at, u - 1)
                            t.wait_ge(a_r2, u - 1)
                            if st > 1 and oc < 2:
                                t.wait_ge(a_en2, (st - 2) * 8 + 7 + oc)
                        else:
                            t.wait_ge(a_ebo, 1)
                            t.wait_ge(a_ebt, NH * 2)
                            t.wait_ge(d_ebt, NH * 2)
                            t.wait_ge(a_ec2, 8)
                        for hh in range(2):
                            h = 2 * oc + hh
                            pr, tp = (slice(0, 64), (0, 0)) if hh == 0 else (slice(64, 128), (0, 64))
                            for mc in range(2):
                                t.matmul(pNM(u)[pr, :],
                                         bufx[:, (h * 2 + mc) * 128:(h * 2 + mc) * 128 + 64],
                                         phq[:, (oc % 2) * 2048 + hh * 1024 + mc * 512:
                                             (oc % 2) * 2048 + hh * 1024 + mc * 512 + 512],
                                         start=(mc == 0), stop=(mc == 1), tile_position=tp)
                            for mc in range(2):
                                ins = t.matmul(pDN(16 + u)[pr, :],
                                               bufx[:, (h * 2 + mc) * 128 + 64:(h * 2 + mc) * 128 + 128],
                                               phq[:, (oc % 2) * 2048 + hh * 1024 + mc * 512:
                                                   (oc % 2) * 2048 + hh * 1024 + mc * 512 + 512],
                                               start=(mc == 0), stop=(mc == 1), tile_position=tp)
                        ins.then_inc(t_nd, 1)

                    if st == 0:
                        emit_buf_c2()
                    for stage in range(10):
                        if stage < 8:
                            emit_qproj(stage)
                        if 1 <= stage <= 8:
                            emit_phiq(stage - 1)
                        if stage >= 2:
                            emit_numden(stage - 2)
                    def emit_tp2(pst):
                        t.wait_ge(a_nrm2, pst + 1)
                        t.wait_ge(a_r2, min((pst + 1) * 8 + 8, NST * 8))
                        for kc in range(8):
                            gi = pst * 8 + kc
                            if gi >= 2:
                                t.wait_ge(a_en2, gi - 1)
                            for sub in range(4):
                                ins = t.matmul(
                                    pDN(32 + gi)[:, sub * 128:sub * 128 + 128],
                                    xn2tm[:, sub * 1024 + kc * 128:sub * 1024 + kc * 128 + 128],
                                    idb[:, :], start=True, stop=True)
                            ins.then_inc(t_tp2, 1)

                    def emit_gate(pst, cst):
                        for oc in range(8):
                            gi = pst * 8 + oc
                            t.wait_ge(a_en2, pst * 8 + 8)
                            t.wait_ge(a_ephq, (cst * 8 + 8) * 4)
                            if gi >= 2:
                                t.wait_ge(a_eg, gi - 1)
                            for kc in range(8):
                                ins = t.matmul(pPH2(32 + gi),
                                               gw1_sb[:, kc * 1024 + oc * 128:kc * 1024 + oc * 128 + 128],
                                               xn2fm[:, kc * 512:kc * 512 + 512],
                                               start=(kc == 0), stop=(kc == 7))
                            ins.then_inc(t_gp, 1)

                    if st > 0:
                        emit_tp2(st - 1)
                        emit_gate(st - 1, st)
                    t.wait_ge(a_ebo, 1)
                    for j in range(8):
                        sub, half = j // 2, j % 2
                        u = st * 8 + j
                        t.wait_ge(d_at, st * 8 + 8)
                        if j >= 2:
                            t.wait_ge(d_x2, u - 1)
                        else:
                            t.wait_ge(a_eq, st * 8 + 7 + j)
                        for kc in range(8):
                            t.matmul(pQP(u),
                                     attn_st[:, kc * 512 + sub * 128:kc * 512 + sub * 128 + 128],
                                     wo_sb[:, kc * 1024 + half * 512:kc * 1024 + half * 512 + 512],
                                     start=(kc == 0), stop=False)
                        ins = t.matmul(pQP(u), ones_row[0:1, 0:128],
                                       bop[0:1, half * 512:half * 512 + 512],
                                       start=False, stop=True)
                        ins.then_inc(t_wo, 1)
                    if st == NST - 1:
                        emit_tp2(st)
                        emit_gate(st, st)
                t.wait_ge(d_hs, 1)
                for kc in range(8):
                    ins = t.matmul(ps_nmB[0:1, 0:NE], hsumb[:, kc:kc + 1],
                                   gw2_sb[:, kc * 8:kc * 8 + 8],
                                   start=(kc == 0), stop=(kc == 7))
                ins.then_inc(t_g, 1)
                t.wait_ge(d_g, 1)
                t.matmul(ps_nmB[0:NE, 0:1], g_row[:, :], onesf_row[0:1, 0:1],
                         start=True, stop=True).then_inc(t_g2, 1)
                t.matmul(ps_dnB[:, 0:NE], onesf_row[:, :], g_row[:, :],
                         start=True, stop=True).then_inc(t_g2, 1)

        _ab.close()

        with contextlib.ExitStack() as _ec:
            mw1t = _ec.enter_context(nc.sbuf_tensor("mw1t", [128, 32768], BF16))
            mw2t = _ec.enter_context(nc.sbuf_tensor("mw2t", [128, 32768], BF16))
            twf = _ec.enter_context(nc.sbuf_tensor("twf", [128, 3 * 5 * 512], F32))
            twg = _ec.enter_context(nc.sbuf_tensor("twg", [128, 2 * 3 * 512], BF16))
            w_sl = _ec.enter_context(nc.sbuf_tensor("w_sl", [128, 512], F32))
            w_sl2 = _ec.enter_context(nc.sbuf_tensor("w_sl2", [128, 512], F32))
            accC = _ec.enter_context(nc.sbuf_tensor("accC", [128, 2 * 512], F32))
            accB2 = _ec.enter_context(nc.sbuf_tensor("accB2", [128, 512], BF16))
            u_stC = _ec.enter_context(nc.sbuf_tensor("u_stC", [128, 8192], BF16))
            xn2C = _ec.enter_context(nc.sbuf_tensor("xn2C", [128, 2048], BF16))
            x2C = _ec.enter_context(nc.sbuf_tensor("x2C", [128, 2048], F32))
            tb2_sb = _ec.enter_context(nc.sbuf_tensor("tb2_sb", [NE, D], BF16))
            b1c_sb = _ec.enter_context(nc.sbuf_tensor("b1c_sb", [128, 32], F32))
            tb1c_sb = _ec.enter_context(nc.sbuf_tensor("tb1c_sb", [128, NE * 32], BF16))
            ps_uC = _ec.enter_context(nc.psum_tensor("ps_uC", [128, 2 * 512], F32))
            ps_oC = _ec.enter_context(nc.psum_tensor("ps_oC", [128, 2 * 512], F32))
            ps_tC = _ec.enter_context(nc.psum_tensor("ps_tC", [128, 2 * 512], F32))
            blk = _ec.enter_context(nc.Block())
            _namc = ("qts0 qts1 qts2 qtg0 qtg1 qtg2 qwsl qwsl2 qxn0 qxn1 qx2c qoc qmisc qdbg qcw "
                     "g_cc3 d_mac a_cst "
                     "t_tpc a_emw d_mb1 t_mb2 a_emb2 t_u a_eu t_o d_o").split()
            _SC = {n: _ec.enter_context(nc.semaphore(n)) for n in _namc}
            (qts0, qts1, qts2, qtg0, qtg1, qtg2, qwsl, qwsl2, qxn0, qxn1, qx2c, qoc, qmisc, qdbg, qcw,
             g_cc3, d_mac, a_cst,
             t_tpc, a_emw, d_mb1, t_mb2, a_emb2, t_u, a_eu, t_o, d_o) = (_SC[n] for n in _namc)
            qtsP = [qts0, qts1, qts2]
            qtgP = [qtg0, qtg1]
            qwsP = [qwsl, qwsl2]
            qxnP = [qxn0, qxn1]

            NU = 32
            ESP = 5

            def wslab_src(js):
                if js < 32:
                    return w1_in[js * 128:(js + 1) * 128, :], 512
                v2 = js - 32
                hc, frp = v2 // 4, v2 % 4
                return w2_in[hc * 128:(hc + 1) * 128, frp * 512:(frp + 1) * 512], 512

            def wslab_idx(u):
                return u, 0

            def tw_slab_ap(u):
                if u < NU:
                    return tw1_in[u, :, :]
                v2 = u - NU
                hc, frp = v2 // 4, v2 % 4
                return tw2_in[hc, frp, :, :]

            def mw_dest_ap(u):
                if u < NU:
                    base = u * 128
                    return bass.AP(mw1t, base, [[32768, 128], [4096, 4], [1, 128]])
                v2 = u - NU
                hc, frp = v2 // 4, v2 % 4
                base = frp * 4 * 1024 + hc * 128
                return bass.AP(mw2t, base, [[32768, 128], [1024, 4], [1, 128]])

            @blk.sync
            def _(s):
                s.wait_ge(a_g2, 1)
                if debug:
                    s.wait_ge(qws, 112)
                s.dma_start(out=b1c_sb[:, :], in_=b1_in).then_inc(qmisc, 16)
                s.wait_ge(qx2s, 16 * NST)
                s.wait_ge(qn2s, 16 * NST)
                s.dma_start(out=xn2C[:, :].rearrange("p (c t) -> p c t", c=8),
                            in_=fm_st_ap(s_xn2, 0, width=256)).then_inc(qxnP[0], 16)
                s.dma_start(out=x2C[:, :].rearrange("p (s d) -> p s d", s=2),
                            in_=x_st_ap(s_x2, 0, width=256)).then_inc(qx2c, 16)
                issue_at = {}
                for js in range(2 * NU):
                    issue_at.setdefault(max(js - 2, -1), []).append(js)
                tw_at = {}
                for u in range(2 * NU):
                    tw_at.setdefault(max(u - 3, -1), []).append(u)

                def issue_wslabs(key):
                    for u in tw_at.get(key, []):
                        base = (u % 3) * ESP * 512
                        e0 = (ESP + 1) // 2
                        s.dma_start(out=twf[:, base:base + e0 * 512],
                                    in_=tw_slab_ap(u)[:, 0:e0 * 512]).then_inc(qtsP[u % 3], 16)
                        s.dma_start(out=twf[:, base + e0 * 512:base + ESP * 512],
                                    in_=tw_slab_ap(u)[:, e0 * 512:ESP * 512]).then_inc(qtsP[u % 3], 16)
                    for js in issue_at.get(key, []):
                        src_ap, wd = wslab_src(js)
                        dst = w_sl if js % 2 == 0 else w_sl2
                        s.dma_start(out=dst[:, 0:wd], in_=src_ap).then_inc(qwsP[js % 2], 16)
                issue_wslabs(-1)
                for u in range(2 * NU):
                    s.wait_ge(d_mac, u + 1)
                    issue_wslabs(u)
                for st in range(NSC):
                    if st >= 1:
                        s.wait_ge(t_u, st * 32)
                        s.dma_start(out=xn2C[:, :].rearrange("p (c t) -> p c t", c=8),
                                    in_=fm_st_ap(s_xn2, st, width=256)).then_inc(qxnP[0], 16)
                    if st >= 1:
                        s.wait_ge(qoc, 16 * st)
                        s.dma_start(out=x2C[:, :].rearrange("p (s d) -> p s d", s=2),
                                    in_=x_st_ap(s_x2, st, width=256)).then_inc(qx2c, 16)
                    s.wait_ge(d_o, st * 4 + 4)
                    s.dma_start(out=x_st_ap(out_d, st, width=256),
                                in_=x2C[:, :].rearrange("p (s d) -> p s d", s=2)).then_inc(qoc, 16)
                s.wait_ge(qoc, 16 * NSC)

            @blk.gpsimd
            def _(g):
                g.wait_ge(a_g2, 1)
                g.wait_ge(qx2s, 16 * NST)
                g.wait_ge(qn2s, 16 * NST)
                if debug:
                    g.wait_ge(qws, 112)
                g.dma_start(out=tb2_sb[:, :], in_=tb2_in).then_inc(qdbg, 16)
                g.dma_start(out=mb2row[0:1, :], in_=b2_in).then_inc(qdbg, 16)
                g.dma_start(out=tb1c_sb[:, :], in_=tb1_in).then_inc(qdbg, 16)
                NG = NE - ESP
                for u in range(2 * NU):
                    if u >= 2:
                        g.wait_ge(d_mac, u - 1)
                    base = (u % 2) * NG * 512
                    g0 = (NG + 1) // 2
                    g.dma_start(out=twg[:, base:base + g0 * 512],
                                in_=tw_slab_ap(u)[:, ESP * 512:(ESP + g0) * 512]).then_inc(qtgP[u % 2], 16)
                    g.dma_start(out=twg[:, base + g0 * 512:base + NG * 512],
                                in_=tw_slab_ap(u)[:, (ESP + g0) * 512:NE * 512]).then_inc(qtgP[u % 2], 16)
                g.wait_ge(a_emw, NU)
                g.dma_start(out=cc_w1_in[0:1, :].rearrange("x (p n) -> (x p) n", p=128),
                            in_=mw1t[:, 0:16384]).then_inc(qcw, 16)
                g.wait_ge(qcw, 16)
                g.collective_compute(
                    "AllGather", ALU.bypass, replica_groups=RG,
                    ins=[cc_w1_in[:, :]], outs=[cc_w1_out[:, :]],
                ).then_inc(g_cc3, 1)
                g.wait_ge(g_cc3, 1)
                g.dma_start(out=mw1t[:, 0:16384],
                            in_=cc_w1_out[0:1, :].rearrange("x (p n) -> (x p) n", p=128)).then_inc(qcw, 16)
                g.dma_start(out=mw1t[:, 16384:32768],
                            in_=cc_w1_out[1:2, :].rearrange("x (p n) -> (x p) n", p=128)).then_inc(qcw, 16)
                g.wait_ge(a_emw, 2 * NU)
                g.wait_ge(qcw, 48)
                g.dma_start(out=cc_w2_in[0:1, :].rearrange("x (p n) -> (x p) n", p=128),
                            in_=mw2t[:, 0:16384]).then_inc(qcw, 16)
                g.wait_ge(qcw, 64)
                g.collective_compute(
                    "AllGather", ALU.bypass, replica_groups=RG,
                    ins=[cc_w2_in[:, :]], outs=[cc_w2_out[:, :]],
                ).then_inc(g_cc3, 1)
                g.wait_ge(g_cc3, 2)
                g.dma_start(out=mw2t[:, 0:16384],
                            in_=cc_w2_out[0:1, :].rearrange("x (p n) -> (x p) n", p=128)).then_inc(qcw, 16)
                g.dma_start(out=mw2t[:, 16384:32768],
                            in_=cc_w2_out[1:2, :].rearrange("x (p n) -> (x p) n", p=128)).then_inc(qcw, 16)
                if debug:
                    g.wait_ge(qcw, 96)
                    g.dma_start(out=dbg["mw1t"].rearrange("(c p) f -> p c f", p=128),
                                in_=mw1t[:, :].rearrange("p (c f) -> p c f", c=8)).then_inc(qdbg, 16)

            @blk.vector
            def _(v):
                v.wait_ge(qmisc, 16)
                v.wait_ge(qdbg, 48)
                v.wait_ge(a_g2, 1)
                if debug:
                    v.wait_ge(qws, 112)
                ss_dep("v", v.scalar_tensor_tensor(mb1c[:, :], tb1c_sb[:, 0:32], g_cols[:, 0:1],
                                                   b1c_sb[:, :], ALU.mult, ALU.add))
                for e in range(1, NE):
                    ss_wait(v, "v")
                    ins = v.scalar_tensor_tensor(mb1c[:, :], tb1c_sb[:, e * 32:(e + 1) * 32],
                                                 g_cols[:, e:e + 1], mb1c[:, :], ALU.mult, ALU.add)
                    if e < NE - 1:
                        ss_dep("v", ins)
                ins.then_inc(d_mb1, 1)
                for u in range(2 * NU):
                    js, wofs = wslab_idx(u)
                    v.wait_ge(qtsP[u % 3], 32 * (u // 3 + 1))
                    v.wait_ge(qwsP[u % 2], 16 * (u // 2 + 1))
                    if u > 0:
                        v.wait_ge(a_cst, u)
                    for e in range(NE):
                        if e == ESP:
                            v.wait_ge(qtgP[u % 2], 32 * (u // 2 + 1))
                        if e < ESP:
                            in0 = twf[:, (u % 3) * ESP * 512 + e * 512:
                                      (u % 3) * ESP * 512 + e * 512 + 512]
                        else:
                            in0 = twg[:, (u % 2) * (NE - ESP) * 512 + (e - ESP) * 512:
                                      (u % 2) * (NE - ESP) * 512 + (e - ESP) * 512 + 512]
                        wbuf = w_sl if js % 2 == 0 else w_sl2
                        in1 = wbuf[:, wofs:wofs + 512] if e == 0 else \
                            accC[:, ((e - 1) % 2) * 512:((e - 1) % 2) * 512 + 512]
                        if e == 0:
                            if u > 0:
                                v.wait_ge(d_mac, u)
                        else:
                            ss_wait(v, "v")
                        ins = v.scalar_tensor_tensor(
                            accC[:, (e % 2) * 512:(e % 2) * 512 + 512],
                            in0, g_cols[:, e:e + 1], in1, ALU.mult, ALU.add)
                        if e < NE - 1:
                            ss_dep("v", ins)
                    ins.then_inc(d_mac, 1)
                v.wait_ge(t_mb2, 2)
                v.tensor_tensor(mb2row[0:1, 0:512], mb2row[0:1, 0:512],
                                ps_oC[0:1, 0:512], ALU.add)
                ins = v.tensor_tensor(mb2row[0:1, 512:1024], mb2row[0:1, 512:1024],
                                      ps_oC[0:1, 512:1024], ALU.add)
                ins.then_inc(a_emb2, 1)
                for st in range(NSC):
                    for j in range(4):
                        sub, half = j // 2, j % 2
                        ou = st * 4 + j
                        v.wait_ge(t_o, ou + 1)
                        v.wait_ge(qx2c, 16 * (st + 1))
                        if j == 0 and st >= 1:
                            v.wait_ge(qoc, 16 * st)
                        ins = v.tensor_tensor(x2C[:, sub * 1024 + half * 512:
                                                  sub * 1024 + half * 512 + 512],
                                              x2C[:, sub * 1024 + half * 512:
                                                  sub * 1024 + half * 512 + 512],
                                              ps_oC[:, (ou % 2) * 512:(ou % 2) * 512 + 512],
                                              ALU.add)
                        ins.then_inc(d_o, 1)

            @blk.scalar
            def _(a):
                if debug:
                    a.wait_ge(qws, 112)
                for u in range(2 * NU):
                    v2 = u % 2
                    a.wait_ge(d_mac, u + 1)
                    a.activation(accB2[:, 0:512],
                                 accC[:, 512:1024], AF.Copy).then_inc(a_cst, 1)
                    a.wait_ge(t_tpc, u + 1)
                    ins = a.activation(mw_dest_ap(u), ps_tC[:, v2 * 512:v2 * 512 + 512],
                                       AF.Copy)
                    ins.then_inc(a_emw, 1)

                for st in range(NSC):
                    for fc in range(32):
                        iu = st * 32 + fc
                        a.wait_ge(t_u, iu + 1)
                        a.wait_ge(d_mb1, 1)
                        if fc == 0 and st >= 1:
                            a.wait_ge(t_o, st * 4)
                        ins = a.activation(u_stC[:, fc * 256:fc * 256 + 256],
                                           ps_uC[:, (iu % 2) * 512:(iu % 2) * 512 + 256],
                                           AF.Gelu_apprx_tanh, bias=mb1c[:, fc:fc + 1])
                        ins.then_inc(a_eu, 1)

            @blk.tensor
            def _(t):
                t.wait_ge(a_g2, 1)
                t.wait_ge(qdbg, 48)
                for half in range(2):
                    ins = t.matmul(ps_oC[0:1, half * 512:half * 512 + 512],
                                   g8[:, :], tb2_sb[:, half * 512:half * 512 + 512],
                                   start=True, stop=True)
                    ins.then_inc(t_mb2, 1)
                for u in range(2 * NU):
                    v2 = u % 2
                    t.wait_ge(a_cst, u + 1)
                    if u >= 2:
                        t.wait_ge(a_emw, u - 1)
                    for b in range(4):
                        ins = t.matmul(ps_tC[:, v2 * 512 + b * 128:v2 * 512 + b * 128 + 128],
                                       accB2[:, b * 128:b * 128 + 128],
                                       idb[:, :], start=True, stop=True)
                    ins.then_inc(t_tpc, 1)
                t.wait_ge(qcw, 48)
                t.wait_ge(a_emb2, 1)
                for st in range(NSC):
                    t.wait_ge(qxnP[0], 16 * (st + 1))
                    for fc in range(32):
                        iu = st * 32 + fc
                        if iu >= 2:
                            t.wait_ge(a_eu, iu - 1)
                        for hc in range(8):
                            ins = t.matmul(ps_uC[:, (iu % 2) * 512:(iu % 2) * 512 + 256],
                                           mw1t[:, hc * 4096 + fc * 128:hc * 4096 + fc * 128 + 128],
                                           xn2C[:, hc * 256:hc * 256 + 256],
                                           start=(hc == 0), stop=(hc == 7))
                        ins.then_inc(t_u, 1)
                    t.wait_ge(a_eu, st * 32 + 32)
                    if st == 0:
                        t.wait_ge(qcw, 96)
                    for j in range(4):
                        sub, half = j // 2, j % 2
                        ou = st * 4 + j
                        if ou >= 2:
                            t.wait_ge(d_o, ou - 1)
                        for fc in range(32):
                            t.matmul(ps_oC[:, (ou % 2) * 512:(ou % 2) * 512 + 512],
                                     u_stC[:, fc * 256 + sub * 128:fc * 256 + sub * 128 + 128],
                                     mw2t[:, fc * 1024 + half * 512:fc * 1024 + half * 512 + 512],
                                     start=(fc == 0), stop=False)
                        ins = t.matmul(ps_oC[:, (ou % 2) * 512:(ou % 2) * 512 + 512],
                                       ones_row[0:1, 0:128],
                                       mb2row[0:1, half * 512:half * 512 + 512],
                                       start=False, stop=True)
                        ins.then_inc(t_o, 1)

        if debug:
            with nc.Block() as blk:
                @blk.sync
                def _(s):
                    s.wait_ge(qx2s, 16 * NST)
                    s.dma_start(out=dbg["x2"], in_=s_x2[:, :]).then_inc(qws, 16)
                    s.wait_ge(d_g, 1)
                    s.dma_start(out=dbg["g"], in_=g_row[:, :]).then_inc(qws, 16)
                    s.wait_ge(qws, 80 + 64)
    return nc


def _prep_core_inputs(inputs, T=4096):
    f32 = np.float32
    x = np.asarray(inputs["L0_attn_x"], dtype=f32)
    B, L, _ = x.shape
    cores = []
    com = {
        "wqT": np.ascontiguousarray(np.asarray(inputs["L0_attn_Wq"], f32).T),
        "wkT": np.ascontiguousarray(np.asarray(inputs["L0_attn_Wk"], f32).T),
        "wvT": np.ascontiguousarray(np.asarray(inputs["L0_attn_Wv"], f32).T),
        "woT": np.ascontiguousarray(np.asarray(inputs["L0_attn_Wo"], f32).T),
        "orfT": np.ascontiguousarray(np.asarray(inputs["L0_attn_orf"], f32).T),
        "bq": np.ascontiguousarray(np.asarray(inputs["L0_attn_bq"], f32).reshape(8, 128).T),
        "bk": np.ascontiguousarray(np.asarray(inputs["L0_attn_bk"], f32).reshape(8, 128).T),
        "bvfm": np.ascontiguousarray(np.asarray(inputs["L0_attn_bv"], f32).reshape(8, 128).T),
        "borow": np.ascontiguousarray(np.asarray(inputs["L0_attn_bo"], f32).reshape(1, D)),
        "gw1T": np.ascontiguousarray(np.asarray(inputs["L0_ffn_gate_w1"], f32).T),
        "gb1": np.ascontiguousarray(np.asarray(inputs["L0_ffn_gate_b1"], f32).reshape(8, 128).T),
        "gw2T": np.ascontiguousarray(np.asarray(inputs["L0_ffn_gate_w2"], f32).T),
        "gb2row": np.ascontiguousarray(np.asarray(inputs["L0_ffn_gate_b2"], f32).reshape(1, NE)),

        "b1c": np.ascontiguousarray(np.asarray(inputs["L0_ffn_b1"], f32).reshape(32, 128).T),
        "tb1c": np.ascontiguousarray(
            np.asarray(inputs["L0_ffn_Tb1"], f32).reshape(NE, 32, 128).transpose(2, 0, 1).reshape(128, NE * 32)),

        "b2row": np.ascontiguousarray(np.asarray(inputs["L0_ffn_b2"], f32).reshape(1, D)),
        "tb2": np.ascontiguousarray(np.asarray(inputs["L0_ffn_Tb2"], f32)),
        "ident": np.eye(128, dtype=f32),
    }
    W1 = np.asarray(inputs["L0_ffn_W1"], f32)
    TW1 = np.asarray(inputs["L0_ffn_TW1"], f32)
    W2 = np.asarray(inputs["L0_ffn_W2"], f32)
    TW2 = np.asarray(inputs["L0_ffn_TW2"], f32)
    halves = []
    for r in range(2):
        t1 = TW1[:, :, r * 512:(r + 1) * 512]
        t1 = np.ascontiguousarray(
            t1.reshape(NE, 32, 128, 512).transpose(1, 2, 0, 3).reshape(32, 128, NE * 512))
        t2 = TW2[:, :, r * 2048:(r + 1) * 2048]
        t2 = np.ascontiguousarray(
            t2.reshape(NE, 8, 128, 4, 512).transpose(1, 3, 2, 0, 4).reshape(8, 4, 128, NE * 512))
        halves.append({
            "w1h": np.ascontiguousarray(W1[:, r * 512:(r + 1) * 512]),
            "tw1h": t1,
            "w2h": np.ascontiguousarray(W2[:, r * 2048:(r + 1) * 2048]),
            "tw2h": t2,
        })
    for b in range(B):
        for h in range(2):
            im = dict(com)
            im.update(halves[h])
            im["x"] = np.ascontiguousarray(x[b, h * T:(h + 1) * T])
            cores.append(im)
    return cores


def kernel(**inputs):
    T = 4096
    nc = build(T=T, L_total=2 * T, debug=False)
    in_maps = _prep_core_inputs(inputs, T)
    res = run_bass_kernel_spmd(nc, in_maps, core_ids=list(range(8)))
    x = np.asarray(inputs["L0_attn_x"], dtype=np.float32)
    B, L, _ = x.shape
    out = np.empty((B, L, D), dtype=np.float32)
    for c in range(8):
        b, h = c // 2, c % 2
        out[b, h * T:(h + 1) * T] = res.results[c]["out"]
    return out


# version 18
# speedup vs baseline: 1.1650x; 1.1650x over previous
import math
import numpy as np

import concourse.bass as bass
import concourse.mybir as mybir
from concourse.bass_utils import run_bass_kernel_spmd

F32 = mybir.dt.float32
BF16 = mybir.dt.bfloat16
AF = mybir.ActivationFunctionType
ALU = mybir.AluOpType
AX = mybir.AxisListType

D = 1024
NH = 16
DH = 64
MF = 256
FH = 4096
NE = 8
EPS_LN = 1e-5
EPS_PHI = 1e-6 / 16.0
NEG_LN16 = -math.log(16.0)
C_ORF = 1.0 / (64.0 ** 0.25)
C_SQ = -1.0 / 16.0


CONSTS = [EPS_LN, NEG_LN16, -0.5, -1.0, EPS_PHI, 1.0, 0.5, -2.0]


def build(T=4096, L_total=8192, debug=False, ncores=8):
    NST = T // 512
    NSC = T // 256
    nc = bass.Bass("TRN2", target_bir_lowering=False)

    x_in = nc.dram_tensor("x", [T, D], F32, kind="ExternalInput").ap()
    wqT = nc.dram_tensor("wqT", [D, D], F32, kind="ExternalInput").ap()
    wkT = nc.dram_tensor("wkT", [D, D], F32, kind="ExternalInput").ap()
    wvT = nc.dram_tensor("wvT", [D, D], F32, kind="ExternalInput").ap()
    woT = nc.dram_tensor("woT", [D, D], F32, kind="ExternalInput").ap()
    orfT_in = nc.dram_tensor("orfT", [DH, MF], F32, kind="ExternalInput").ap()
    bq_in = nc.dram_tensor("bq", [128, 8], F32, kind="ExternalInput").ap()
    bk_in = nc.dram_tensor("bk", [128, 8], F32, kind="ExternalInput").ap()
    bv_in = nc.dram_tensor("bvfm", [128, 8], F32, kind="ExternalInput").ap()
    bo_in = nc.dram_tensor("borow", [1, D], F32, kind="ExternalInput").ap()
    gw1T = nc.dram_tensor("gw1T", [D, D], F32, kind="ExternalInput").ap()
    gb1_in = nc.dram_tensor("gb1", [128, 8], F32, kind="ExternalInput").ap()
    gw2T_in = nc.dram_tensor("gw2T", [D, NE], F32, kind="ExternalInput").ap()
    gb2_in = nc.dram_tensor("gb2row", [1, NE], F32, kind="ExternalInput").ap()
    w1_in = nc.dram_tensor("w1h", [FH, 512], F32, kind="ExternalInput").ap()
    tw1_in = nc.dram_tensor("tw1h", [32, 128, NE * 512], F32, kind="ExternalInput").ap()
    b1_in = nc.dram_tensor("b1c", [128, 32], F32, kind="ExternalInput").ap()
    tb1_in = nc.dram_tensor("tb1c", [128, NE * 32], F32, kind="ExternalInput").ap()
    w2_in = nc.dram_tensor("w2h", [D, 2048], F32, kind="ExternalInput").ap()
    tw2_in = nc.dram_tensor("tw2h", [8, 4, 128, NE * 512], F32, kind="ExternalInput").ap()
    b2_in = nc.dram_tensor("b2row", [1, D], F32, kind="ExternalInput").ap()
    tb2_in = nc.dram_tensor("tb2", [NE, D], F32, kind="ExternalInput").ap()
    ident_in = nc.dram_tensor("ident", [128, 128], F32, kind="ExternalInput").ap()

    out_d = nc.dram_tensor("out", [T, D], F32, kind="ExternalOutput").ap()

    s_xn1 = nc.dram_tensor("s_xn1", [D, T], BF16).ap()
    s_xn2 = nc.dram_tensor("s_xn2", [D, T], BF16).ap()
    s_x2 = nc.dram_tensor("s_x2", [T, D], F32).ap()
    CCA = 65 * NH * MF + NH * 65
    cc_a_in = nc.dram_tensor("cc_a_in", [1, CCA], F32)
    cc_a_out = nc.dram_tensor("cc_a_out", [1, CCA], F32)
    cc_g_in = nc.dram_tensor("cc_g_in", [1, NE], F32)
    cc_g_out = nc.dram_tensor("cc_g_out", [1, NE], F32)
    NWH = 128 * 16384
    cc_w1_in = nc.dram_tensor("cc_w1_in", [1, NWH], BF16)
    cc_w1_out = nc.dram_tensor("cc_w1_out", [2, NWH], BF16)
    cc_w2_in = nc.dram_tensor("cc_w2_in", [1, NWH], BF16)
    cc_w2_out = nc.dram_tensor("cc_w2_out", [2, NWH], BF16)
    RG = [[i, i + 1] for i in range(0, ncores, 2)]

    dbg = {}
    if debug:
        dbg["kv_acc"] = nc.dram_tensor("d_kvacc", [65, NH * MF], F32, kind="ExternalOutput").ap()
        dbg["vsum"] = nc.dram_tensor("d_vsum", [1, NH * 65], F32, kind="ExternalOutput").ap()
        dbg["xn1"] = nc.dram_tensor("d_xn1", [D, T], F32, kind="ExternalOutput").ap()
        dbg["x2"] = nc.dram_tensor("d_x2", [T, D], F32, kind="ExternalOutput").ap()
        dbg["g"] = nc.dram_tensor("d_g", [1, NE], F32, kind="ExternalOutput").ap()
        dbg["mw1t"] = nc.dram_tensor("d_mw1t", [D, FH], F32, kind="ExternalOutput").ap()

    def x_st_ap(dram, st, width=512):
        return dram[st * width:(st + 1) * width, :].rearrange("(s p) d -> p s d", p=128)

    def fm_st_ap(dram, st, width=512):
        return dram[:, st * width:(st + 1) * width].rearrange("(c p) t -> p c t", p=128)

    import contextlib
    with contextlib.ExitStack() as _es:
        idb = _es.enter_context(nc.sbuf_tensor("idb", [128, 128], BF16))
        cpool = _es.enter_context(nc.sbuf_tensor("cpool", [128, 8], F32))
        ones_row = _es.enter_context(nc.sbuf_tensor("ones_row", [1, 512], BF16))
        ones_col = _es.enter_context(nc.sbuf_tensor("ones_col", [128, 1], BF16))
        onef_col = _es.enter_context(nc.sbuf_tensor("onef_col", [128, 1], F32))
        g_row = _es.enter_context(nc.sbuf_tensor("g_row", [1, NE], F32))
        g_cols = _es.enter_context(nc.sbuf_tensor("g_cols", [128, NE], F32))
        g8 = _es.enter_context(nc.sbuf_tensor("g8", [NE, 1], F32))
        mb1c = _es.enter_context(nc.sbuf_tensor("mb1c", [128, 32], F32))
        mb2row = _es.enter_context(nc.sbuf_tensor("mb2row", [1, D], BF16))
        ccg_sb = _es.enter_context(nc.sbuf_tensor("ccg_sb", [1, NE], F32))
        qxA0 = _es.enter_context(nc.semaphore("qxA0"))
        qxA1 = _es.enter_context(nc.semaphore("qxA1"))
        qxP = [qxA0, qxA1]
        qsp0 = _es.enter_context(nc.semaphore("qsp0"))
        qsp1 = _es.enter_context(nc.semaphore("qsp1"))
        qsP = [qsp0, qsp1]
        qwa = _es.enter_context(nc.semaphore("qwa"))
        qwg = _es.enter_context(nc.semaphore("qwg"))
        qws = _es.enter_context(nc.semaphore("qws"))
        a_sq = _es.enter_context(nc.semaphore("a_sq"))
        d_st = _es.enter_context(nc.semaphore("d_st"))
        a_inv = _es.enter_context(nc.semaphore("a_inv"))
        d_nm = _es.enter_context(nc.semaphore("d_nm"))
        a_nrm = _es.enter_context(nc.semaphore("a_nrm"))
        t_tp = _es.enter_context(nc.semaphore("t_tp"))
        a_etp = _es.enter_context(nc.semaphore("a_etp"))
        t_kp = _es.enter_context(nc.semaphore("t_kp"))
        a_ek = _es.enter_context(nc.semaphore("a_ek"))
        t_vp = _es.enter_context(nc.semaphore("t_vp"))
        a_ev = _es.enter_context(nc.semaphore("a_ev"))
        t_phi = _es.enter_context(nc.semaphore("t_phi"))
        a_ephi = _es.enter_context(nc.semaphore("a_ephi"))
        t_kvs = _es.enter_context(nc.semaphore("t_kvs"))
        d_kv = _es.enter_context(nc.semaphore("d_kv"))
        d_xs = _es.enter_context(nc.semaphore("d_xs"))
        t_vs = _es.enter_context(nc.semaphore("t_vs"))
        a_evs = _es.enter_context(nc.semaphore("a_evs"))
        g_cc = _es.enter_context(nc.semaphore("g_cc"))
        qcc = _es.enter_context(nc.semaphore("qcc"))
        d_pre = _es.enter_context(nc.semaphore("d_pre"))
        bar = _es.enter_context(nc.semaphore("bar"))
        t_bt = _es.enter_context(nc.semaphore("t_bt"))
        a_ebt = _es.enter_context(nc.semaphore("a_ebt"))
        d_ebt = _es.enter_context(nc.semaphore("d_ebt"))
        t_c2 = _es.enter_context(nc.semaphore("t_c2"))
        a_ec2 = _es.enter_context(nc.semaphore("a_ec2"))
        d_dc = _es.enter_context(nc.semaphore("d_dc"))
        t_bo = _es.enter_context(nc.semaphore("t_bo"))
        a_ebo = _es.enter_context(nc.semaphore("a_ebo"))
        onesf_row = _es.enter_context(nc.sbuf_tensor("onesf_row", [1, 128], F32))
        dss = _es.enter_context(nc.semaphore("dss"))
        ass = _es.enter_context(nc.semaphore("ass"))
        _ss = {"v": [dss, 0], "a": [ass, 0]}

        def ss_dep(key, ins):
            _ss[key][1] += 1
            ins.then_inc(_ss[key][0], 1)

        def ss_wait(eng, key):
            eng.wait_ge(_ss[key][0], _ss[key][1])
        gb2_sb = _es.enter_context(nc.sbuf_tensor("gb2_sb", [1, NE], F32))
        _names_b = ("qxB0 qxB1 qnB0 qnB1 qx2s qn2s qwb t_qp a_eq t_phq a_ephq "
                    "t_nd a_r2 d_at t_wo d_x2 a_sq2 d_st2 a_inv2 d_nm2 a_nrm2 "
                    "t_tp2 a_etp2 a_en2 t_gp a_eg d_hs t_g a_gv qcg d_g t_g2 "
                    "a_g2 g_cc2").split()
        _S = {n: _es.enter_context(nc.semaphore(n)) for n in _names_b}
        (qxB0, qxB1, qnB0, qnB1, qx2s, qn2s, qwb, t_qp, a_eq, t_phq, a_ephq,
         t_nd, a_r2, d_at, t_wo, d_x2, a_sq2, d_st2, a_inv2, d_nm2, a_nrm2,
         t_tp2, a_etp2, a_en2, t_gp, a_eg, d_hs, t_g, a_gv, qcg, d_g, t_g2,
         a_g2, g_cc2) = (_S[n] for n in _names_b)
        qxBP = [qxB0, qxB1]
        qnBP = [qnB0, qnB1]
        for i, cv in enumerate(CONSTS):
            nc.const_aps.register(F32, cv, cpool[:, i:i + 1])
        _ab = contextlib.ExitStack()
        idf = _ab.enter_context(nc.sbuf_tensor("idf", [128, 128], F32))
        orfpad = _ab.enter_context(nc.sbuf_tensor("orfpad", [128, 512], BF16))
        sqpad = _ab.enter_context(nc.sbuf_tensor("sqpad", [128, 512], BF16))
        bkq = _ab.enter_context(nc.sbuf_tensor("bkq", [128, 16], F32))
        gb1c = _ab.enter_context(nc.sbuf_tensor("gb1c", [128, 8], F32))
        bvb = _ab.enter_context(nc.sbuf_tensor("bvb", [128, 8], BF16))
        borow_b = _ab.enter_context(nc.sbuf_tensor("borow_b", [1, D], BF16))
        bop = _ab.enter_context(nc.sbuf_tensor("boprow", [1, D], BF16))
        kv_acc = _ab.enter_context(nc.sbuf_tensor("kv_acc", [65, NH * MF], F32))
        vsum_sb = _ab.enter_context(nc.sbuf_tensor("vsum_sb", [1, NH * 65], F32))
        cv_sb = _ab.enter_context(nc.sbuf_tensor("cv_sb", [1, NH * 65], F32))
        xn1s = _ab.enter_context(nc.sbuf_tensor("xn1s", [128, NST * 8], F32))
        xn1sum = _ab.enter_context(nc.sbuf_tensor("xn1sum", [128, 8], F32))
        xn1sum_b = _ab.enter_context(nc.sbuf_tensor("xn1sum_b", [128, 8], BF16))
        bufx = _ab.enter_context(nc.sbuf_tensor("bufx", [128, NH * 2 * 128], BF16))
        c2cols = _ab.enter_context(nc.sbuf_tensor("c2cols", [128, 8], F32))
        dencols = _ab.enter_context(nc.sbuf_tensor("dencols", [128, 8], F32))
        hsum = _ab.enter_context(nc.sbuf_tensor("hsum", [128, 8 * 8 + 8], F32))
        orff = _ab.enter_context(nc.sbuf_tensor("orff", [128, 512], F32))
        with (
            nc.Block() as blk,
        ):
            @blk.sync
            def _(s):
                s.dma_start(out=idf[:, :], in_=ident_in).then_inc(qws, 16)
                s.dma_start(out=bkq[:, 0:8], in_=bq_in).then_inc(qws, 16)
                s.dma_start(out=bkq[:, 8:16], in_=bk_in).then_inc(qws, 16)
                s.dma_start(out=gb1c[:, :], in_=gb1_in).then_inc(qws, 16)
                s.dma_start(out=gb2_sb[:, :], in_=gb2_in).then_inc(qws, 16)

            @blk.gpsimd
            def _(g):
                for i, cv in enumerate(CONSTS):
                    g.memset(cpool[:, i:i + 1], cv)
                g.memset(ones_row[:, :], 1.0)
                g.memset(ones_col[:, :], 1.0)
                g.memset(onef_col[:, :], 1.0)
                g.memset(onesf_row[:, :], 1.0)
                g.memset(vsum_sb[0:1, :].rearrange("p (h x) -> p h x", x=65)[:, :, 64:65], float(T))
                g.memset(orff[:, :], 0.0)
                g.memset(sqpad[:, :], 0.0)
                g.drain()
                g.memset(sqpad[0:64, 0:256], C_SQ)
                g.memset(sqpad[64:128, 256:512], C_SQ)
                g.drain()
                g.dma_start(out=idb[:, :], in_=ident_in).then_inc(qwg, 16)
                g.dma_start(out=bvb[:, :], in_=bv_in).then_inc(qwg, 16)
                g.dma_start(out=borow_b[:, :], in_=bo_in).then_inc(qwg, 16)
                g.drain()
                g.dma_start(out=orff[0:64, 0:256], in_=orfT_in).then_inc(qwg, 16)
                g.dma_start(out=orff[64:128, 256:512], in_=orfT_in).then_inc(qwg, 16)

            @blk.vector
            def _(v):
                v.wait_ge(qwg, 80)
                v.tensor_scalar_mul(orfpad[:, :], orff[:, :], C_ORF).then_inc(d_pre, 1)

        with (
            nc.sbuf_tensor("xtA", [128, 2 * 4096], F32) as xtA,
            nc.sbuf_tensor("statA", [128, 2 * 24], F32) as statA,
            nc.sbuf_tensor("xn1tm", [128, 2 * 4096], BF16) as xn1tm,
            nc.sbuf_tensor("xn1fm", [128, 2 * 4096], BF16) as xn1fm,
            nc.sbuf_tensor("wk_sb", [128, 8192], BF16) as wk_sb,
            nc.sbuf_tensor("wv_sb", [128, 8192], BF16) as wv_sb,
            nc.sbuf_tensor("kk_st", [128, 2 * 4096], BF16) as kk_st,
            nc.sbuf_tensor("ks_st", [128, 2 * 4096], BF16) as ks_st,
            nc.sbuf_tensor("vp_st", [128, 2 * 4160], BF16) as vp_st,
            nc.sbuf_tensor("phk", [128, 2 * 2048], BF16) as phk,
            nc.sbuf_tensor("sq_sink", [128, 1024], F32) as sq_sink,
            nc.psum_tensor("ps_tpA", [128, 2 * 512], F32) as ps_tpA,
            nc.psum_tensor("ps_pjA", [128, 2 * 512], F32) as ps_pjA,
            nc.psum_tensor("ps_phA", [128, 2 * 512], F32) as ps_phA,
            nc.psum_tensor("ps_kvA", [128, 2 * 512], F32) as ps_kvA,
            nc.Block() as blk,
        ):
            def pTP(i):
                return ps_tpA[:, (i % 2) * 512:(i % 2) * 512 + 512]

            def pPJ(i):
                return ps_pjA[:, (i % 2) * 512:(i % 2) * 512 + 512]

            def pPH(i):
                return ps_phA[:, (i % 2) * 512:(i % 2) * 512 + 512]

            def pKV(i):
                return ps_kvA[:, (i % 2) * 512:(i % 2) * 512 + 512]

            @blk.sync
            def _(s):
                for st in range(min(2, NST)):
                    s.dma_start(out=xtA[:, (st % 2) * 4096:(st % 2) * 4096 + 4096],
                                in_=x_st_ap(x_in, st)).then_inc(qxP[st % 2], 16)
                for st in range(NST):
                    if st + 2 < NST:
                        s.wait_ge(a_nrm, st + 1)
                        s.dma_start(out=xtA[:, (st % 2) * 4096:(st % 2) * 4096 + 4096],
                                    in_=x_st_ap(x_in, st + 2)).then_inc(qxP[st % 2], 16)
                    s.wait_ge(a_etp, st * 8 + 8)
                    s.dma_start(out=fm_st_ap(s_xn1, st),
                                in_=xn1fm[:, (st % 2) * 4096:(st % 2) * 4096 + 4096]
                                .rearrange("p (c t) -> p c t", c=8)).then_inc(qsP[st % 2], 16)

            @blk.gpsimd
            def _(g):
                g.memset(vp_st[:, :].rearrange("p (b h x) -> p b h x", x=65, h=16)[:, :, :, 64:65], 1.0)
                g.dma_start(out=wk_sb[:, :],
                            in_=wkT[:, :].rearrange("(c p) o -> p c o", p=128)).then_inc(qwa, 16)
                g.dma_start(out=wv_sb[:, :],
                            in_=wvT[:, :].rearrange("(c p) o -> p c o", p=128)).then_inc(qwa, 16)

            @blk.scalar
            def _(a):
                for st in range(NST):
                    p = st % 2
                    xt = xtA[:, p * 4096:p * 4096 + 4096]
                    stat = statA[:, p * 24:p * 24 + 24]
                    if st == 0:
                        a.wait_ge(qws, 80)
                    a.wait_ge(qxP[st % 2], 16 * (st // 2 + 1))
                    for sub in range(4):
                        if sub == 0:
                            if st > 0:
                                a.wait_ge(a_sq, st)
                        else:
                            ss_wait(a, "a")
                        ins = a.activation(
                            sq_sink[:, :],
                            xt[:, sub * 1024:sub * 1024 + 1024],
                            AF.Square, accum_out=stat[:, 4 + sub:5 + sub])
                        if sub == 3:
                            ins.then_inc(a_sq, 1)
                        else:
                            ss_dep("a", ins)
                    a.wait_ge(d_st, st + 1)
                    ss_dep("a", a.activation(stat[:, 12:16], stat[:, 8:12], AF.Ln, bias=EPS_LN))
                    ss_wait(a, "a")
                    a.activation(stat[:, 16:20], stat[:, 12:16], AF.Exp, scale=-0.5).then_inc(a_inv, 1)
                    a.wait_ge(d_nm, st + 1)
                    if st >= 2:
                        a.wait_ge(t_tp, (st - 2) * 8 + 8)
                    for sub in range(4):
                        ins = a.activation(
                            xn1tm[:, p * 4096 + sub * 1024: p * 4096 + sub * 1024 + 1024],
                            xt[:, sub * 1024:sub * 1024 + 1024],
                            AF.Identity, bias=stat[:, 20 + sub:21 + sub],
                            scale=stat[:, 16 + sub:17 + sub])
                        if sub == 3:
                            ins.then_inc(a_nrm, 1)
                    for kc in range(8):
                        gi = st * 8 + kc
                        a.wait_ge(t_tp, gi + 1)
                        if st >= 2:
                            a.wait_ge(qsP[st % 2], 16 * ((st - 2) // 2 + 1))
                            a.wait_ge(t_vp, (st - 2) * 8 + 8)
                        ins = a.activation(
                            xn1fm[:, p * 4096 + kc * 512: p * 4096 + kc * 512 + 512],
                            pTP(gi), AF.Copy,
                            accum_out=xn1s[:, st * 8 + kc:st * 8 + kc + 1])
                        ins.then_inc(a_etp, 1)
                    for oc in range(8):
                        gi = st * 8 + oc
                        a.wait_ge(t_kp, gi + 1)
                        if st >= 2:
                            a.wait_ge(t_phi, (st - 2) * 32 + 32)
                        a.activation(kk_st[:, p * 4096 + oc * 512: p * 4096 + oc * 512 + 512],
                                     pPJ(gi), AF.Identity, bias=bkq[:, 8 + oc:9 + oc])
                        ins = a.activation(ks_st[:, p * 4096 + oc * 512: p * 4096 + oc * 512 + 512],
                                           pPJ(gi), AF.Square, bias=bkq[:, 8 + oc:9 + oc])
                        ins.then_inc(a_ek, 1)
                    for j in range(8):
                        sub, half = j // 2, j % 2
                        gi = st * 8 + j
                        a.wait_ge(t_vp, gi + 1)
                        if st >= 2:
                            a.wait_ge(t_kvs, (st - 2) * 8 + 8)
                        dst = vp_st[:, p * 4160 + sub * 1040 + half * 520:
                                    p * 4160 + sub * 1040 + half * 520 + 520] \
                            .rearrange("p (h x) -> p h x", x=65)[:, :, 0:64]
                        ins = a.activation(dst, pPJ(8 + gi), AF.Copy)
                        ins.then_inc(a_ev, 1)
                    for oc in range(8):
                        for sub in range(4):
                            fi = st * 32 + oc * 4 + sub
                            a.wait_ge(t_phi, fi + 1)
                            if sub == 0 and st * 8 + oc >= 2:
                                a.wait_ge(t_kvs, st * 8 + oc - 1)
                            ins = a.activation(
                                phk[:, (oc % 2) * 2048 + sub * 512:
                                    (oc % 2) * 2048 + sub * 512 + 512],
                                pPH(fi), AF.Exp, bias=NEG_LN16)
                            ins.then_inc(a_ephi, 1)
                a.wait_ge(t_vs, 1)
                for half in range(2):
                    dst = vsum_sb[0:1, half * 520:half * 520 + 520] \
                        .rearrange("p (h x) -> p h x", x=65)[:, :, 0:64]
                    ins = a.activation(dst, ps_pjA[0:1, half * 512:half * 512 + 512], AF.Copy)
                ins.then_inc(a_evs, 1)

            @blk.vector
            def _(v):
                for st in range(NST):
                    p = st % 2
                    xt = xtA[:, p * 4096:p * 4096 + 4096]
                    stat = statA[:, p * 24:p * 24 + 24]
                    v.wait_ge(qxP[st % 2], 16 * (st // 2 + 1))
                    for sub in range(4):
                        ss_dep("v", v.reduce_sum(stat[:, sub:sub + 1],
                                                 xt[:, sub * 1024:sub * 1024 + 1024], AX.X))
                    ss_wait(v, "v")
                    ss_dep("v", v.tensor_scalar_mul(stat[:, 0:4], stat[:, 0:4], 1.0 / D))
                    v.wait_ge(a_sq, st + 1)
                    ss_wait(v, "v")
                    ss_dep("v", v.tensor_tensor(stat[:, 12:16], stat[:, 0:4], stat[:, 0:4], ALU.mult))
                    ss_wait(v, "v")
                    v.scalar_tensor_tensor(stat[:, 8:12], stat[:, 4:8], 1.0 / D,
                                           stat[:, 12:16], ALU.mult, ALU.subtract).then_inc(d_st, 1)
                    v.wait_ge(a_inv, st + 1)
                    v.scalar_tensor_tensor(stat[:, 20:24], stat[:, 0:4], -1.0,
                                           stat[:, 16:20], ALU.mult, ALU.mult).then_inc(d_nm, 1)
                    for oc in range(8):
                        if st > 0:
                            v.wait_ge(d_kv, (st - 1) * 8 + oc + 1)
                        v.wait_ge(t_kvs, st * 8 + oc + 1)
                        dst = kv_acc[:, oc * 2 * MF:(oc + 1) * 2 * MF]
                        if st == 0:
                            ins = v.tensor_copy(dst, ps_kvA[0:65, (oc % 2) * 512:(oc % 2) * 512 + 512])
                        else:
                            ins = v.tensor_tensor(dst, dst,
                                                  ps_kvA[0:65, (oc % 2) * 512:(oc % 2) * 512 + 512],
                                                  ALU.add)
                        ins.then_inc(d_kv, 1)
                v.wait_ge(a_etp, NST * 8)
                for kc in range(8):
                    ss_dep("v", v.reduce_sum(xn1sum[:, kc:kc + 1], xn1s[:, kc:NST * 8:8], AX.X))
                ss_wait(v, "v")
                v.tensor_copy(xn1sum_b[:, :], xn1sum[:, :]).then_inc(d_xs, 1)

            @blk.tensor
            def _(t):
                t.wait_ge(qwa, 32)
                t.wait_ge(qwg, 80)
                t.wait_ge(d_pre, 1)
                for st in range(NST):
                    p = st % 2
                    t.wait_ge(a_nrm, st + 1)
                    for kc in range(8):
                        gi = st * 8 + kc
                        if gi >= 2:
                            t.wait_ge(a_etp, gi - 1)
                        for sub in range(4):
                            ins = t.matmul(
                                pTP(gi)[:, sub * 128:sub * 128 + 128],
                                xn1tm[:, p * 4096 + sub * 1024 + kc * 128:
                                      p * 4096 + sub * 1024 + kc * 128 + 128],
                                idb[:, :], start=True, stop=True)
                        ins.then_inc(t_tp, 1)
                    t.wait_ge(a_etp, st * 8 + 8)
                    for oc in range(8):
                        gi = st * 8 + oc
                        if oc >= 2:
                            t.wait_ge(a_ek, gi - 1)
                        elif st > 0:
                            t.wait_ge(a_ev, (st - 1) * 8 + 7 + oc)
                        for kc in range(8):
                            ins = t.matmul(pPJ(gi),
                                           wk_sb[:, kc * 1024 + oc * 128: kc * 1024 + oc * 128 + 128],
                                           xn1fm[:, p * 4096 + kc * 512: p * 4096 + kc * 512 + 512],
                                           start=(kc == 0), stop=(kc == 7))
                        ins.then_inc(t_kp, 1)
                    for j in range(8):
                        sub, half = j // 2, j % 2
                        gi = st * 8 + j
                        if j >= 2:
                            t.wait_ge(a_ev, gi - 1)
                        else:
                            t.wait_ge(a_ek, st * 8 + 7 + j)
                        for kc in range(8):
                            ins = t.matmul(pPJ(8 + gi),
                                           xn1fm[:, p * 4096 + kc * 512 + sub * 128:
                                                 p * 4096 + kc * 512 + sub * 128 + 128],
                                           wv_sb[:, kc * 1024 + half * 512: kc * 1024 + half * 512 + 512],
                                           start=(kc == 0), stop=(kc == 7))
                        ins.then_inc(t_vp, 1)
                    def emit_phi(oc, st=st, p=p):
                        for sub in range(4):
                            fi = st * 32 + oc * 4 + sub
                            t.wait_ge(a_ek, st * 8 + oc + 1)
                            if fi >= 2:
                                t.wait_ge(a_ephi, fi - 1)
                            t.matmul(pPH(fi),
                                     kk_st[:, p * 4096 + oc * 512 + sub * 128:
                                           p * 4096 + oc * 512 + sub * 128 + 128],
                                     orfpad[:, :], start=True, stop=False)
                            ins = t.matmul(pPH(fi),
                                           ks_st[:, p * 4096 + oc * 512 + sub * 128:
                                                 p * 4096 + oc * 512 + sub * 128 + 128],
                                           sqpad[:, :], start=False, stop=True)
                            ins.then_inc(t_phi, 1)

                    def emit_kv(oc, st=st, p=p):
                        t.wait_ge(a_ephi, st * 32 + oc * 4 + 4)
                        t.wait_ge(a_ev, st * 8 + 8)
                        if st * 8 + oc >= 2:
                            t.wait_ge(d_kv, st * 8 + oc - 1)
                        for hh in range(2):
                            for sub in range(4):
                                ins = t.matmul(
                                    pKV(oc)[0:65, hh * 256:hh * 256 + 256],
                                    vp_st[:, p * 4160 + sub * 1040 + (2 * oc + hh) * 65:
                                          p * 4160 + sub * 1040 + (2 * oc + hh) * 65 + 65],
                                    phk[:, (oc % 2) * 2048 + sub * 512 + hh * 256:
                                        (oc % 2) * 2048 + sub * 512 + hh * 256 + 256],
                                    start=(sub == 0), stop=(sub == 3))
                        ins.then_inc(t_kvs, 1)

                    for oc in range(9):
                        if oc < 8:
                            emit_phi(oc)
                        if oc >= 1:
                            emit_kv(oc - 1)
                t.wait_ge(d_xs, 1)
                t.wait_ge(a_ev, NST * 8)
                for half in range(2):
                    for kc in range(8):
                        ins = t.matmul(ps_pjA[0:1, half * 512:half * 512 + 512],
                                       xn1sum_b[:, kc:kc + 1],
                                       wv_sb[:, kc * 1024 + half * 512: kc * 1024 + half * 512 + 512],
                                       start=(kc == 0), stop=(kc == 7))
                ins.then_inc(t_vs, 1)


        with contextlib.ExitStack() as _eb:
            xtB = _eb.enter_context(nc.sbuf_tensor("xtB", [128, 2 * 4096], F32))
            statB = _eb.enter_context(nc.sbuf_tensor("statB", [128, 2 * 24], F32))
            xn1fB = _eb.enter_context(nc.sbuf_tensor("xn1fB", [128, 2 * 4096], BF16))
            wq_sb = _eb.enter_context(nc.sbuf_tensor("wq_sb", [128, 8192], BF16))
            wo_sb = _eb.enter_context(nc.sbuf_tensor("wo_sb", [128, 8192], BF16))
            gw1_sb = _eb.enter_context(nc.sbuf_tensor("gw1_sb", [128, 8192], BF16))
            gw2_sb = _eb.enter_context(nc.sbuf_tensor("gw2_sb", [128, 64], BF16))
            qq_sl = _eb.enter_context(nc.sbuf_tensor("qq_sl", [128, 2 * 512], BF16))
            qs_sl = _eb.enter_context(nc.sbuf_tensor("qs_sl", [128, 2 * 512], BF16))
            phq = _eb.enter_context(nc.sbuf_tensor("phq", [128, 2 * 2048], BF16))
            attn_st = _eb.enter_context(nc.sbuf_tensor("attn_st", [128, 4096], BF16))
            x2st = _eb.enter_context(nc.sbuf_tensor("x2st", [128, 4096], F32))
            xn2tm = _eb.enter_context(nc.sbuf_tensor("xn2tm", [128, 4096], BF16))
            xn2fm = _eb.enter_context(nc.sbuf_tensor("xn2fm", [128, 4096], BF16))
            r2_sb = _eb.enter_context(nc.sbuf_tensor("r2_sb", [128, 2 * 512], F32))
            sinkB = _eb.enter_context(nc.sbuf_tensor("sinkB", [128, 2 * 512], BF16))
            sqsB = _eb.enter_context(nc.sbuf_tensor("sqsB", [128, 1024], F32))
            hsumb = _eb.enter_context(nc.sbuf_tensor("hsumb", [128, 8], BF16))
            ps_qpB = _eb.enter_context(nc.psum_tensor("ps_qpB", [128, 2 * 512], F32))
            ps_phB = _eb.enter_context(nc.psum_tensor("ps_phB", [128, 2 * 512], F32))
            ps_nmB = _eb.enter_context(nc.psum_tensor("ps_nmB", [128, 2 * 512], F32))
            ps_dnB = _eb.enter_context(nc.psum_tensor("ps_dnB", [128, 2 * 512], F32))
            blk = _eb.enter_context(nc.Block())
            def pQP(i):
                return ps_qpB[:, (i % 2) * 512:(i % 2) * 512 + 512]

            def pPH2(i):
                return ps_phB[:, (i % 2) * 512:(i % 2) * 512 + 512]

            def pNM(i):
                return ps_nmB[:, (i % 2) * 512:(i % 2) * 512 + 512]

            def pDN(i):
                return ps_dnB[:, (i % 2) * 512:(i % 2) * 512 + 512]

            NSPA0 = 16 * ((NST + 1) // 2)
            NSPA1 = 16 * (NST // 2)

            @blk.sync
            def _(s):
                if debug:
                    s.wait_ge(qcc, 64)
                    s.dma_start(out=dbg["kv_acc"], in_=kv_acc[:, :]).then_inc(qws, 16)
                    s.dma_start(out=dbg["vsum"], in_=vsum_sb[:, :]).then_inc(qws, 16)
                s.wait_ge(qsP[0], NSPA0)
                if NST > 1:
                    s.wait_ge(qsP[1], NSPA1)
                for st in range(min(2, NST)):
                    s.dma_start(out=xn1fB[:, (st % 2) * 4096:(st % 2) * 4096 + 4096]
                                .rearrange("p (c t) -> p c t", c=8),
                                in_=fm_st_ap(s_xn1, st)).then_inc(qnBP[st % 2], 16)
                    s.dma_start(out=xtB[:, (st % 2) * 4096:(st % 2) * 4096 + 4096],
                                in_=x_st_ap(x_in, st)).then_inc(qxBP[st % 2], 16)
                for st in range(NST):
                    p = st % 2
                    if st + 2 < NST:
                        s.wait_ge(t_qp, st * 8 + 8)
                        s.dma_start(out=xn1fB[:, p * 4096:p * 4096 + 4096]
                                    .rearrange("p (c t) -> p c t", c=8),
                                    in_=fm_st_ap(s_xn1, st + 2)).then_inc(qnBP[p], 16)
                        s.wait_ge(d_x2, st * 8 + 8)
                        s.dma_start(out=xtB[:, p * 4096:p * 4096 + 4096],
                                    in_=x_st_ap(x_in, st + 2)).then_inc(qxBP[p], 16)
                    s.wait_ge(d_x2, st * 8 + 8)
                    s.dma_start(out=x_st_ap(s_x2, st), in_=x2st[:, :]
                                .rearrange("p (s d) -> p s d", s=4)).then_inc(qx2s, 16)
                    s.wait_ge(a_en2, st * 8 + 8)
                    s.dma_start(out=fm_st_ap(s_xn2, st), in_=xn2fm[:, :]
                                .rearrange("p (c t) -> p c t", c=8)).then_inc(qn2s, 16)

            @blk.gpsimd
            def _(g):
                g.wait_ge(d_kv, NST * 8)
                g.wait_ge(a_evs, 1)
                g.wait_ge(t_vs, 1)
                g.wait_ge(d_xs, 1)
                g.wait_ge(qsP[0], NSPA0)
                if NST > 1:
                    g.wait_ge(qsP[1], NSPA1)
                g.dma_start(out=cc_a_in[0:1, 0:65 * NH * MF].rearrange("x (p m) -> (x p) m", p=65),
                            in_=kv_acc[:, :]).then_inc(qcc, 16)
                g.dma_start(out=cc_a_in[0:1, 65 * NH * MF:CCA], in_=vsum_sb[:, :]).then_inc(qcc, 16)
                g.dma_start(out=wq_sb[:, :],
                            in_=wqT[:, :].rearrange("(c p) o -> p c o", p=128)).then_inc(qwb, 16)
                g.dma_start(out=wo_sb[:, :],
                            in_=woT[:, :].rearrange("(c p) o -> p c o", p=128)).then_inc(qwb, 16)
                g.dma_start(out=gw1_sb[:, :],
                            in_=gw1T[:, :].rearrange("(c p) o -> p c o", p=128)).then_inc(qwb, 16)
                g.dma_start(out=gw2_sb[:, :],
                            in_=gw2T_in[:, :].rearrange("(c p) e -> p c e", p=128)).then_inc(qwb, 16)
                g.wait_ge(qcc, 32)
                g.collective_compute(
                    "AllReduce", ALU.add, replica_groups=RG,
                    ins=[cc_a_in[:, :]], outs=[cc_a_out[:, :]],
                ).then_inc(g_cc, 1)
                g.wait_ge(g_cc, 1)
                g.dma_start(out=kv_acc[:, :],
                            in_=cc_a_out[0:1, 0:65 * NH * MF].rearrange("x (p m) -> (x p) m", p=65)).then_inc(qcc, 16)
                g.dma_start(out=vsum_sb[:, :], in_=cc_a_out[0:1, 65 * NH * MF:CCA]).then_inc(qcc, 16)
                g.wait_ge(a_gv, 1)
                g.dma_start(out=cc_g_in[:, :], in_=ccg_sb[:, :]).then_inc(qcg, 16)
                g.wait_ge(qcg, 16)
                g.collective_compute(
                    "AllReduce", ALU.add, replica_groups=RG,
                    ins=[cc_g_in[:, :]], outs=[cc_g_out[:, :]],
                ).then_inc(g_cc2, 1)
                g.wait_ge(g_cc2, 1)
                g.dma_start(out=ccg_sb[:, :], in_=cc_g_out[:, :]).then_inc(qcg, 16)

            @blk.vector
            def _(v):
                v.wait_ge(qcc, 64)
                v.tensor_scalar_mul(cv_sb[:, :], vsum_sb[:, :], EPS_PHI).then_inc(d_dc, 1)
                for h in range(NH):
                    for mc in range(2):
                        i = h * 2 + mc
                        v.wait_ge(a_ebt, i + 1)
                        ins = v.tensor_copy(
                            bufx[:, i * 128 + 64:i * 128 + 128],
                            pDN(i)[:, 64:65].broadcast_to((128, 64)))
                        ins.then_inc(d_ebt, 1)
                for st in range(NST):
                    p = st % 2
                    for oc in range(8):
                        u = st * 8 + oc
                        v.wait_ge(t_nd, u + 1)
                        v.wait_ge(a_r2, u + 1)
                        if st > 0 and oc == 0:
                            v.wait_ge(t_wo, st * 8)
                        ins = v.scalar_tensor_tensor(attn_st[:, oc * 512:oc * 512 + 512],
                                                     pNM(u), c2cols[:, oc:oc + 1],
                                                     r2_sb[:, (u % 2) * 512:(u % 2) * 512 + 512],
                                                     ALU.add, ALU.mult)
                        ins.then_inc(d_at, 1)
                    if st > 0:
                        v.wait_ge(d_st2, st)
                    for j in range(8):
                        sub, half = j // 2, j % 2
                        u = st * 8 + j
                        v.wait_ge(t_wo, u + 1)
                        v.wait_ge(qxBP[p], 16 * (st // 2 + 1))
                        if st > 0 and j == 0:
                            v.wait_ge(qx2s, 16 * st)
                            v.wait_ge(a_nrm2, st)
                        ins = v.tensor_tensor(x2st[:, sub * 1024 + half * 512:
                                                   sub * 1024 + half * 512 + 512],
                                              xtB[:, p * 4096 + sub * 1024 + half * 512:
                                                  p * 4096 + sub * 1024 + half * 512 + 512],
                                              pQP(u), ALU.add)
                        ins.then_inc(d_x2, 1)
                    stat = statB[:, p * 24:p * 24 + 24]
                    v.wait_ge(d_x2, st * 8 + 8)
                    for sub in range(4):
                        ss_dep("v", v.reduce_sum(stat[:, sub:sub + 1],
                                                 x2st[:, sub * 1024:sub * 1024 + 1024], AX.X))
                    ss_wait(v, "v")
                    ss_dep("v", v.tensor_scalar_mul(stat[:, 0:4], stat[:, 0:4], 1.0 / D))
                    v.wait_ge(a_sq2, st + 1)
                    ss_wait(v, "v")
                    ss_dep("v", v.tensor_tensor(stat[:, 12:16], stat[:, 0:4], stat[:, 0:4], ALU.mult))
                    ss_wait(v, "v")
                    v.scalar_tensor_tensor(stat[:, 8:12], stat[:, 4:8], 1.0 / D,
                                           stat[:, 12:16], ALU.mult, ALU.subtract).then_inc(d_st2, 1)
                    v.wait_ge(a_inv2, st + 1)
                    v.scalar_tensor_tensor(stat[:, 20:24], stat[:, 0:4], -1.0,
                                           stat[:, 16:20], ALU.mult, ALU.mult).then_inc(d_nm2, 1)
                v.wait_ge(a_eg, NST * 8)
                for oc in range(8):
                    ss_dep("v", v.reduce_sum(hsum[:, 64 + oc:65 + oc],
                                             hsum[:, oc * NST:(oc + 1) * NST], AX.X))
                ss_wait(v, "v")
                v.tensor_copy(hsumb[:, :], hsum[:, 64:72]).then_inc(d_hs, 1)
                v.wait_ge(qcg, 32)
                v.scalar_tensor_tensor(g_row[:, :], ccg_sb[:, :], 1.0 / L_total,
                                       gb2_sb[:, :], ALU.mult, ALU.add).then_inc(d_g, 1)

            @blk.scalar
            def _(a):
                def a_emit_buf_c2():
                    for h in range(NH):
                        for mc in range(2):
                            i = h * 2 + mc
                            a.wait_ge(t_bt, i + 1)
                            ins = a.activation(bufx[:, i * 128:i * 128 + 64],
                                               pDN(i)[:, 0:64], AF.Copy)
                            ins.then_inc(a_ebt, 1)
                    for pr in range(8):
                        a.wait_ge(t_c2, pr + 1)
                        a.activation(c2cols[:, pr:pr + 1], ps_qpB[:, 0:1],
                                     AF.Identity, scale=EPS_PHI)
                        ins = a.activation(dencols[:, pr:pr + 1], ps_qpB[:, 1:2],
                                           AF.Identity, scale=EPS_PHI)
                        ins.then_inc(a_ec2, 1)
                a.wait_ge(t_bo, 2)
                a.activation(bop[0:1, 0:512], ps_nmB[0:1, 0:512], AF.Copy)
                a.activation(bop[0:1, 512:1024], ps_nmB[0:1, 512:1024], AF.Copy).then_inc(a_ebo, 1)
                for st in range(NST):
                    p = st % 2
                    stat = statB[:, p * 24:p * 24 + 24]
                    def a_emit_qevac(oc, st=st, p=p):
                        u = st * 8 + oc
                        a.wait_ge(t_qp, u + 1)
                        if u >= 2:
                            a.wait_ge(t_phq, (u - 2) * 4 + 4)
                        a.activation(qq_sl[:, (oc % 2) * 512:(oc % 2) * 512 + 512],
                                     pQP(u), AF.Identity, bias=bkq[:, oc:oc + 1])
                        ins = a.activation(qs_sl[:, (oc % 2) * 512:(oc % 2) * 512 + 512],
                                           pQP(u), AF.Square, bias=bkq[:, oc:oc + 1])
                        ins.then_inc(a_eq, 1)

                    def a_emit_phiq(oc, st=st, p=p):
                        for z in range(4):
                            u4 = (st * 8 + oc) * 4 + z
                            a.wait_ge(t_phq, u4 + 1)
                            if st * 8 + oc >= 2 and z == 0:
                                a.wait_ge(t_nd, st * 8 + oc - 1)
                            ins = a.activation(
                                phq[:, (oc % 2) * 2048 + z * 512:(oc % 2) * 2048 + z * 512 + 512],
                                pPH2(u4), AF.Exp, bias=NEG_LN16)
                            ins.then_inc(a_ephq, 1)

                    def a_emit_r2(oc, st=st, p=p):
                        u = st * 8 + oc
                        a.wait_ge(t_nd, u + 1)
                        ss_dep("a", a.activation(r2_sb[:, (u % 2) * 512:(u % 2) * 512 + 512],
                                                 pDN(16 + u)[:, :], AF.Ln,
                                                 bias=dencols[:, oc:oc + 1]))
                        ss_wait(a, "a")
                        ins = a.activation(r2_sb[:, (u % 2) * 512:(u % 2) * 512 + 512],
                                           r2_sb[:, (u % 2) * 512:(u % 2) * 512 + 512],
                                           AF.Exp, scale=-1.0)
                        ins.then_inc(a_r2, 1)

                    if st == 0:
                        a_emit_buf_c2()
                    for stage in range(10):
                        if stage < 8:
                            a_emit_qevac(stage)
                        if 1 <= stage <= 8:
                            a_emit_phiq(stage - 1)
                        if stage >= 2:
                            a_emit_r2(stage - 2)

                    def a_emit_xn2gate(pst):
                        for kc in range(8):
                            gi = pst * 8 + kc
                            a.wait_ge(t_tp2, gi + 1)
                            if pst >= 1 and kc == 0:
                                a.wait_ge(t_gp, pst * 8)
                                a.wait_ge(qn2s, 16 * pst)
                            ins = a.activation(xn2fm[:, kc * 512:kc * 512 + 512],
                                               pDN(32 + gi), AF.Copy)
                            ins.then_inc(a_en2, 1)
                        for oc in range(8):
                            gi = pst * 8 + oc
                            a.wait_ge(t_gp, gi + 1)
                            if gi >= 2:
                                a.wait_ge(a_eg, gi - 1)
                            ins = a.activation(sinkB[:, (oc % 2) * 512:(oc % 2) * 512 + 512],
                                               pPH2(32 + gi), AF.Relu, bias=gb1c[:, oc:oc + 1],
                                               accum_out=hsum[:, oc * NST + pst:oc * NST + pst + 1])
                            ins.then_inc(a_eg, 1)

                    if st > 0:
                        a_emit_xn2gate(st - 1)

                    a.wait_ge(d_x2, st * 8 + 8)
                    stat = statB[:, p * 24:p * 24 + 24]
                    for sub in range(4):
                        if sub == 0:
                            a.wait_ge(a_sq2, st)
                        else:
                            ss_wait(a, "a")
                        ins = a.activation(sqsB[:, :], x2st[:, sub * 1024:sub * 1024 + 1024],
                                           AF.Square, accum_out=stat[:, 4 + sub:5 + sub])
                        if sub == 3:
                            ins.then_inc(a_sq2, 1)
                        else:
                            ss_dep("a", ins)
                    a.wait_ge(d_st2, st + 1)
                    ss_dep("a", a.activation(stat[:, 12:16], stat[:, 8:12], AF.Ln, bias=EPS_LN))
                    ss_wait(a, "a")
                    a.activation(stat[:, 16:20], stat[:, 12:16], AF.Exp, scale=-0.5).then_inc(a_inv2, 1)
                    a.wait_ge(d_nm2, st + 1)
                    if st >= 1:
                        a.wait_ge(t_tp2, st * 8)
                    for sub in range(4):
                        ins = a.activation(
                            xn2tm[:, sub * 1024:sub * 1024 + 1024],
                            x2st[:, sub * 1024:sub * 1024 + 1024],
                            AF.Identity, bias=stat[:, 20 + sub:21 + sub],
                            scale=stat[:, 16 + sub:17 + sub])
                        if sub == 3:
                            ins.then_inc(a_nrm2, 1)
                    if st == NST - 1:
                        a_emit_xn2gate(st)
                a.wait_ge(t_g, 1)
                a.activation(ccg_sb[:, :], ps_nmB[0:1, 0:NE], AF.Copy).then_inc(a_gv, 1)
                a.wait_ge(t_g2, 2)
                a.activation(g8[:, :], ps_nmB[0:NE, 0:1], AF.Copy)
                ins = a.activation(g_cols[:, :], ps_dnB[:, 0:NE], AF.Copy)
                ins.then_inc(a_g2, 1)

            @blk.tensor
            def _(t):
                t.wait_ge(qwb, 64)
                for half in range(2):
                    for kc in range(8):
                        t.matmul(ps_nmB[0:1, half * 512:half * 512 + 512],
                                 bvb[:, kc:kc + 1],
                                 wo_sb[:, kc * 1024 + half * 512:kc * 1024 + half * 512 + 512],
                                 start=(kc == 0), stop=False)
                    ins = t.matmul(ps_nmB[0:1, half * 512:half * 512 + 512],
                                   ones_row[0:1, 0:1], borow_b[0:1, half * 512:half * 512 + 512],
                                   start=False, stop=True)
                    ins.then_inc(t_bo, 1)
                def emit_buf_c2():
                    t.wait_ge(d_dc, 1)
                    for h in range(NH):
                        for mc in range(2):
                            i = h * 2 + mc
                            if i >= 2:
                                t.wait_ge(a_ebt, i - 1)
                                t.wait_ge(d_ebt, i - 1)
                            t.matmul(pDN(i)[:, 0:65],
                                     kv_acc[:, h * MF + mc * 128:h * MF + mc * 128 + 128],
                                     idf[0:65, 0:65], start=True, stop=False)
                            ins = t.matmul(pDN(i)[:, 0:65],
                                           onesf_row[:, :], cv_sb[0:1, h * 65:(h + 1) * 65],
                                           start=False, stop=True)
                            ins.then_inc(t_bt, 1)
                    for pr in range(8):
                        hA, hB = 2 * pr, 2 * pr + 1
                        t.wait_ge(a_ebt, (hB + 1) * 2)
                        t.wait_ge(d_ebt, (hB + 1) * 2)
                        if pr >= 1:
                            t.wait_ge(a_ec2, pr)
                        for hh, h in ((0, hA), (1, hB)):
                            tp = (0, 0) if hh == 0 else (0, 64)
                            prr = slice(0, 64) if hh == 0 else slice(64, 128)
                            t.matmul(ps_qpB[prr, 0:1], bufx[:, (h * 2) * 128:(h * 2) * 128 + 64],
                                     ones_col[:, :], start=True, stop=False, tile_position=tp)
                            t.matmul(ps_qpB[prr, 0:1], bufx[:, (h * 2 + 1) * 128:(h * 2 + 1) * 128 + 64],
                                     ones_col[:, :], start=False, stop=True, tile_position=tp)
                            t.matmul(ps_qpB[prr, 1:2], bufx[:, (h * 2) * 128 + 64:(h * 2) * 128 + 128],
                                     ones_col[:, :], start=True, stop=False, tile_position=tp)
                            ins = t.matmul(ps_qpB[prr, 1:2],
                                           bufx[:, (h * 2 + 1) * 128 + 64:(h * 2 + 1) * 128 + 128],
                                           ones_col[:, :], start=False, stop=True, tile_position=tp)
                        ins.then_inc(t_c2, 1)
                for st in range(NST):
                    p = st % 2
                    t.wait_ge(qnBP[p], 16 * (st // 2 + 1))
                    def emit_qproj(oc, st=st, p=p):
                        u = st * 8 + oc
                        if u == 0:
                            t.wait_ge(a_ec2, 8)
                        if u >= 2:
                            t.wait_ge(a_eq, u - 1)
                        if st > 0 and oc < 2:
                            t.wait_ge(d_x2, (st - 1) * 8 + 7 + oc)
                        for kc in range(8):
                            ins = t.matmul(pQP(u),
                                           wq_sb[:, kc * 1024 + oc * 128:kc * 1024 + oc * 128 + 128],
                                           xn1fB[:, p * 4096 + kc * 512:p * 4096 + kc * 512 + 512],
                                           start=(kc == 0), stop=(kc == 7))
                        ins.then_inc(t_qp, 1)

                    def emit_phiq(oc, st=st, p=p):
                        u = st * 8 + oc
                        t.wait_ge(a_eq, u + 1)
                        for z in range(4):
                            hh, mc = z // 2, z % 2
                            u4 = u * 4 + z
                            if st > 1 and oc == 0 and z < 2:
                                t.wait_ge(a_eg, (st - 1) * 8)
                            if u4 >= 2:
                                t.wait_ge(a_ephq, u4 - 1)
                            t.matmul(pPH2(u4),
                                     orfpad[:, hh * 256 + mc * 128:hh * 256 + mc * 128 + 128],
                                     qq_sl[:, (oc % 2) * 512:(oc % 2) * 512 + 512],
                                     start=True, stop=False)
                            ins = t.matmul(pPH2(u4),
                                           sqpad[:, hh * 256 + mc * 128:hh * 256 + mc * 128 + 128],
                                           qs_sl[:, (oc % 2) * 512:(oc % 2) * 512 + 512],
                                           start=False, stop=True)
                            ins.then_inc(t_phq, 1)

                    def emit_numden(oc, st=st, p=p):
                        u = st * 8 + oc
                        t.wait_ge(a_ephq, u * 4 + 4)
                        if u >= 2:
                            t.wait_ge(d_at, u - 1)
                            t.wait_ge(a_r2, u - 1)
                            if st > 1 and oc < 2:
                                t.wait_ge(a_en2, (st - 2) * 8 + 7 + oc)
                        else:
                            t.wait_ge(a_ebo, 1)
                            t.wait_ge(a_ebt, NH * 2)
                            t.wait_ge(d_ebt, NH * 2)
                            t.wait_ge(a_ec2, 8)
                        for hh in range(2):
                            h = 2 * oc + hh
                            pr, tp = (slice(0, 64), (0, 0)) if hh == 0 else (slice(64, 128), (0, 64))
                            for mc in range(2):
                                t.matmul(pNM(u)[pr, :],
                                         bufx[:, (h * 2 + mc) * 128:(h * 2 + mc) * 128 + 64],
                                         phq[:, (oc % 2) * 2048 + hh * 1024 + mc * 512:
                                             (oc % 2) * 2048 + hh * 1024 + mc * 512 + 512],
                                         start=(mc == 0), stop=(mc == 1), tile_position=tp)
                            for mc in range(2):
                                ins = t.matmul(pDN(16 + u)[pr, :],
                                               bufx[:, (h * 2 + mc) * 128 + 64:(h * 2 + mc) * 128 + 128],
                                               phq[:, (oc % 2) * 2048 + hh * 1024 + mc * 512:
                                                   (oc % 2) * 2048 + hh * 1024 + mc * 512 + 512],
                                               start=(mc == 0), stop=(mc == 1), tile_position=tp)
                        ins.then_inc(t_nd, 1)

                    if st == 0:
                        emit_buf_c2()
                    for stage in range(10):
                        if stage < 8:
                            emit_qproj(stage)
                        if 1 <= stage <= 8:
                            emit_phiq(stage - 1)
                        if stage >= 2:
                            emit_numden(stage - 2)
                    def emit_tp2(pst):
                        t.wait_ge(a_nrm2, pst + 1)
                        t.wait_ge(a_r2, min((pst + 1) * 8 + 8, NST * 8))
                        for kc in range(8):
                            gi = pst * 8 + kc
                            if gi >= 2:
                                t.wait_ge(a_en2, gi - 1)
                            for sub in range(4):
                                ins = t.matmul(
                                    pDN(32 + gi)[:, sub * 128:sub * 128 + 128],
                                    xn2tm[:, sub * 1024 + kc * 128:sub * 1024 + kc * 128 + 128],
                                    idb[:, :], start=True, stop=True)
                            ins.then_inc(t_tp2, 1)

                    def emit_gate(pst, cst):
                        for oc in range(8):
                            gi = pst * 8 + oc
                            t.wait_ge(a_en2, pst * 8 + 8)
                            t.wait_ge(a_ephq, (cst * 8 + 8) * 4)
                            if gi >= 2:
                                t.wait_ge(a_eg, gi - 1)
                            for kc in range(8):
                                ins = t.matmul(pPH2(32 + gi),
                                               gw1_sb[:, kc * 1024 + oc * 128:kc * 1024 + oc * 128 + 128],
                                               xn2fm[:, kc * 512:kc * 512 + 512],
                                               start=(kc == 0), stop=(kc == 7))
                            ins.then_inc(t_gp, 1)

                    if st > 0:
                        emit_tp2(st - 1)
                        emit_gate(st - 1, st)
                    t.wait_ge(a_ebo, 1)
                    for j in range(8):
                        sub, half = j // 2, j % 2
                        u = st * 8 + j
                        t.wait_ge(d_at, st * 8 + 8)
                        if j >= 2:
                            t.wait_ge(d_x2, u - 1)
                        else:
                            t.wait_ge(a_eq, st * 8 + 7 + j)
                        for kc in range(8):
                            t.matmul(pQP(u),
                                     attn_st[:, kc * 512 + sub * 128:kc * 512 + sub * 128 + 128],
                                     wo_sb[:, kc * 1024 + half * 512:kc * 1024 + half * 512 + 512],
                                     start=(kc == 0), stop=False)
                        ins = t.matmul(pQP(u), ones_row[0:1, 0:128],
                                       bop[0:1, half * 512:half * 512 + 512],
                                       start=False, stop=True)
                        ins.then_inc(t_wo, 1)
                    if st == NST - 1:
                        emit_tp2(st)
                        emit_gate(st, st)
                t.wait_ge(d_hs, 1)
                for kc in range(8):
                    ins = t.matmul(ps_nmB[0:1, 0:NE], hsumb[:, kc:kc + 1],
                                   gw2_sb[:, kc * 8:kc * 8 + 8],
                                   start=(kc == 0), stop=(kc == 7))
                ins.then_inc(t_g, 1)
                t.wait_ge(d_g, 1)
                t.matmul(ps_nmB[0:NE, 0:1], g_row[:, :], onesf_row[0:1, 0:1],
                         start=True, stop=True).then_inc(t_g2, 1)
                t.matmul(ps_dnB[:, 0:NE], onesf_row[:, :], g_row[:, :],
                         start=True, stop=True).then_inc(t_g2, 1)

        _ab.close()

        with contextlib.ExitStack() as _ec:
            mw1t = _ec.enter_context(nc.sbuf_tensor("mw1t", [128, 32768], BF16))
            mw2t = _ec.enter_context(nc.sbuf_tensor("mw2t", [128, 32768], BF16))
            tw_sl = _ec.enter_context(nc.sbuf_tensor("tw_sl", [128, 3 * 4096], BF16))
            w_sl = _ec.enter_context(nc.sbuf_tensor("w_sl", [128, 1024], F32))
            w_sl2 = _ec.enter_context(nc.sbuf_tensor("w_sl2", [128, 1024], F32))
            accC = _ec.enter_context(nc.sbuf_tensor("accC", [128, 2 * 512], F32))
            accB2 = _ec.enter_context(nc.sbuf_tensor("accB2", [128, 2 * 512], BF16))
            u_stC = _ec.enter_context(nc.sbuf_tensor("u_stC", [128, 8192], BF16))
            xn2C = _ec.enter_context(nc.sbuf_tensor("xn2C", [128, 2048], BF16))
            x2C = _ec.enter_context(nc.sbuf_tensor("x2C", [128, 2048], F32))
            tb2_sb = _ec.enter_context(nc.sbuf_tensor("tb2_sb", [NE, D], F32))
            b2r_sb = _ec.enter_context(nc.sbuf_tensor("b2r_sb", [1, D], F32))
            b1c_sb = _ec.enter_context(nc.sbuf_tensor("b1c_sb", [128, 32], F32))
            tb1c_sb = _ec.enter_context(nc.sbuf_tensor("tb1c_sb", [128, NE * 32], F32))
            ps_uC = _ec.enter_context(nc.psum_tensor("ps_uC", [128, 2 * 512], F32))
            ps_oC = _ec.enter_context(nc.psum_tensor("ps_oC", [128, 2 * 512], F32))
            ps_tC = _ec.enter_context(nc.psum_tensor("ps_tC", [128, 2 * 512], F32))
            blk = _ec.enter_context(nc.Block())
            _namc = ("qtw0 qtw1 qtw2 qwsl qxn0 qxn1 qx2c qoc qmisc qdbg qcw g_cc3 d_mac a_cst "
                     "t_tpc a_emw d_mb1 t_mb2 a_emb2 t_u a_eu t_o d_o").split()
            _SC = {n: _ec.enter_context(nc.semaphore(n)) for n in _namc}
            (qtw0, qtw1, qtw2, qwsl, qxn0, qxn1, qx2c, qoc, qmisc, qdbg, qcw, g_cc3, d_mac, a_cst,
             t_tpc, a_emw, d_mb1, t_mb2, a_emb2, t_u, a_eu, t_o, d_o) = (_SC[n] for n in _namc)
            qtwP = [qtw0, qtw1, qtw2]
            qxnP = [qxn0, qxn1]

            NU = 32

            def wslab_src(js):
                if js < 32:
                    return w1_in[js * 128:(js + 1) * 128, :], 512
                v2 = js - 32
                hc, q = v2 // 2, v2 % 2
                return w2_in[hc * 128:(hc + 1) * 128, q * 1024:(q + 1) * 1024], 1024

            def wslab_idx(u):
                if u < NU:
                    return u, 0
                v2 = u - NU
                hc, frp = v2 // 4, v2 % 4
                return 32 + hc * 2 + frp // 2, (frp % 2) * 512

            def tw_slab_ap(u):
                if u < NU:
                    return tw1_in[u, :, :]
                v2 = u - NU
                hc, frp = v2 // 4, v2 % 4
                return tw2_in[hc, frp, :, :]

            def mw_dest_ap(u):
                if u < NU:
                    base = u * 128
                    return bass.AP(mw1t, base, [[32768, 128], [4096, 4], [1, 128]])
                v2 = u - NU
                hc, frp = v2 // 4, v2 % 4
                base = frp * 4 * 1024 + hc * 128
                return bass.AP(mw2t, base, [[32768, 128], [1024, 4], [1, 128]])

            @blk.sync
            def _(s):
                s.wait_ge(a_g2, 1)
                if debug:
                    s.wait_ge(qws, 112)
                s.dma_start(out=tb2_sb[:, :], in_=tb2_in).then_inc(qmisc, 16)
                s.dma_start(out=b2r_sb[:, :], in_=b2_in).then_inc(qmisc, 16)
                s.dma_start(out=b1c_sb[:, :], in_=b1_in).then_inc(qmisc, 16)
                s.dma_start(out=tb1c_sb[:, :], in_=tb1_in).then_inc(qmisc, 16)
                s.wait_ge(qx2s, 16 * NST)
                s.wait_ge(qn2s, 16 * NST)
                s.dma_start(out=xn2C[:, :].rearrange("p (c t) -> p c t", c=8),
                            in_=fm_st_ap(s_xn2, 0, width=256)).then_inc(qxnP[0], 16)
                s.dma_start(out=x2C[:, :].rearrange("p (s d) -> p s d", s=2),
                            in_=x_st_ap(s_x2, 0, width=256)).then_inc(qx2c, 16)
                first_unit = {js: (js if js < 32 else 32 + ((js - 32) // 2) * 4 + ((js - 32) % 2) * 2)
                              for js in range(48)}
                issue_at = {}
                for js in range(48):
                    issue_at.setdefault(max(first_unit[js] - 2, -1), []).append(js)
                _nws = [0]

                def issue_wslabs(key):
                    for js in issue_at.get(key, []):
                        if _nws[0] >= 2:
                            s.wait_ge(qwsl, 16 * _nws[0])
                        src_ap, wd = wslab_src(js)
                        dst = w_sl if js % 2 == 0 else w_sl2
                        s.dma_start(out=dst[:, 0:wd], in_=src_ap).then_inc(qwsl, 16)
                        _nws[0] += 1
                issue_wslabs(-1)
                for u in range(2 * NU):
                    s.wait_ge(d_mac, u + 1)
                    issue_wslabs(u)
                for st in range(NSC):
                    if st >= 1:
                        s.wait_ge(t_u, st * 32)
                        s.dma_start(out=xn2C[:, :].rearrange("p (c t) -> p c t", c=8),
                                    in_=fm_st_ap(s_xn2, st, width=256)).then_inc(qxnP[0], 16)
                    if st >= 1:
                        s.wait_ge(qoc, 16 * st)
                        s.dma_start(out=x2C[:, :].rearrange("p (s d) -> p s d", s=2),
                                    in_=x_st_ap(s_x2, st, width=256)).then_inc(qx2c, 16)
                    s.wait_ge(d_o, st * 4 + 4)
                    s.dma_start(out=x_st_ap(out_d, st, width=256),
                                in_=x2C[:, :].rearrange("p (s d) -> p s d", s=2)).then_inc(qoc, 16)
                s.wait_ge(qoc, 16 * NSC)

            @blk.gpsimd
            def _(g):
                g.wait_ge(t_g, 1)
                g.wait_ge(qx2s, 16 * NST)
                g.wait_ge(qn2s, 16 * NST)
                if debug:
                    g.wait_ge(qws, 112)
                for u in range(2 * NU):
                    if u >= 3:
                        g.wait_ge(d_mac, u - 2)
                    g.dma_start(out=tw_sl[:, (u % 3) * 4096:(u % 3) * 4096 + 512 * NE],
                                in_=tw_slab_ap(u)).then_inc(qtwP[u % 3], 16)
                g.wait_ge(a_emw, NU)
                g.dma_start(out=cc_w1_in[0:1, :].rearrange("x (p n) -> (x p) n", p=128),
                            in_=mw1t[:, 0:16384]).then_inc(qcw, 16)
                g.wait_ge(qcw, 16)
                g.collective_compute(
                    "AllGather", ALU.bypass, replica_groups=RG,
                    ins=[cc_w1_in[:, :]], outs=[cc_w1_out[:, :]],
                ).then_inc(g_cc3, 1)
                g.wait_ge(g_cc3, 1)
                g.dma_start(out=mw1t[:, 0:16384],
                            in_=cc_w1_out[0:1, :].rearrange("x (p n) -> (x p) n", p=128)).then_inc(qcw, 16)
                g.dma_start(out=mw1t[:, 16384:32768],
                            in_=cc_w1_out[1:2, :].rearrange("x (p n) -> (x p) n", p=128)).then_inc(qcw, 16)
                g.wait_ge(a_emw, 2 * NU)
                g.wait_ge(qcw, 48)
                g.dma_start(out=cc_w2_in[0:1, :].rearrange("x (p n) -> (x p) n", p=128),
                            in_=mw2t[:, 0:16384]).then_inc(qcw, 16)
                g.wait_ge(qcw, 64)
                g.collective_compute(
                    "AllGather", ALU.bypass, replica_groups=RG,
                    ins=[cc_w2_in[:, :]], outs=[cc_w2_out[:, :]],
                ).then_inc(g_cc3, 1)
                g.wait_ge(g_cc3, 2)
                g.dma_start(out=mw2t[:, 0:16384],
                            in_=cc_w2_out[0:1, :].rearrange("x (p n) -> (x p) n", p=128)).then_inc(qcw, 16)
                g.dma_start(out=mw2t[:, 16384:32768],
                            in_=cc_w2_out[1:2, :].rearrange("x (p n) -> (x p) n", p=128)).then_inc(qcw, 16)
                if debug:
                    g.wait_ge(qcw, 96)
                    g.dma_start(out=dbg["mw1t"].rearrange("(c p) f -> p c f", p=128),
                                in_=mw1t[:, :].rearrange("p (c f) -> p c f", c=8)).then_inc(qdbg, 16)

            @blk.vector
            def _(v):
                v.wait_ge(qmisc, 64)
                v.wait_ge(a_g2, 1)
                if debug:
                    v.wait_ge(qws, 112)
                ss_dep("v", v.scalar_tensor_tensor(mb1c[:, :], tb1c_sb[:, 0:32], g_cols[:, 0:1],
                                                   b1c_sb[:, :], ALU.mult, ALU.add))
                for e in range(1, NE):
                    ss_wait(v, "v")
                    ins = v.scalar_tensor_tensor(mb1c[:, :], tb1c_sb[:, e * 32:(e + 1) * 32],
                                                 g_cols[:, e:e + 1], mb1c[:, :], ALU.mult, ALU.add)
                    if e < NE - 1:
                        ss_dep("v", ins)
                ins.then_inc(d_mb1, 1)
                for u in range(2 * NU):
                    js, wofs = wslab_idx(u)
                    v.wait_ge(qtwP[u % 3], 16 * (u // 3 + 1))
                    v.wait_ge(qwsl, 16 * (2 if js <= 1 else js + 1))
                    if u > 0:
                        v.wait_ge(a_cst, u)
                    for e in range(NE):
                        in0 = tw_sl[:, (u % 3) * 4096 + e * 512:(u % 3) * 4096 + e * 512 + 512]
                        wbuf = w_sl if js % 2 == 0 else w_sl2
                        in1 = wbuf[:, wofs:wofs + 512] if e == 0 else \
                            accC[:, ((e - 1) % 2) * 512:((e - 1) % 2) * 512 + 512]
                        if e == 0:
                            if u > 0:
                                v.wait_ge(d_mac, u)
                        else:
                            ss_wait(v, "v")
                        ins = v.scalar_tensor_tensor(
                            accC[:, (e % 2) * 512:(e % 2) * 512 + 512],
                            in0, g_cols[:, e:e + 1], in1, ALU.mult, ALU.add)
                        if e < NE - 1:
                            ss_dep("v", ins)
                    ins.then_inc(d_mac, 1)
                for st in range(NSC):
                    for j in range(4):
                        sub, half = j // 2, j % 2
                        ou = st * 4 + j
                        v.wait_ge(t_o, ou + 1)
                        v.wait_ge(qx2c, 16 * (st + 1))
                        if j == 0 and st >= 1:
                            v.wait_ge(qoc, 16 * st)
                        ins = v.tensor_tensor(x2C[:, sub * 1024 + half * 512:
                                                  sub * 1024 + half * 512 + 512],
                                              x2C[:, sub * 1024 + half * 512:
                                                  sub * 1024 + half * 512 + 512],
                                              ps_oC[:, (ou % 2) * 512:(ou % 2) * 512 + 512],
                                              ALU.add)
                        ins.then_inc(d_o, 1)

            @blk.scalar
            def _(a):
                if debug:
                    a.wait_ge(qws, 112)
                for u in range(2 * NU):
                    v2 = u % 2
                    a.wait_ge(d_mac, u + 1)
                    if u >= 2:
                        a.wait_ge(t_tpc, u - 1)
                    a.activation(accB2[:, v2 * 512:v2 * 512 + 512],
                                 accC[:, 512:1024], AF.Copy).then_inc(a_cst, 1)
                    a.wait_ge(t_tpc, u + 1)
                    ins = a.activation(mw_dest_ap(u), ps_tC[:, v2 * 512:v2 * 512 + 512],
                                       AF.Copy)
                    ins.then_inc(a_emw, 1)
                a.wait_ge(t_mb2, 2)
                a.activation(mb2row[0:1, 0:512], ps_oC[0:1, 0:512], AF.Copy)
                ins = a.activation(mb2row[0:1, 512:1024], ps_oC[0:1, 512:1024], AF.Copy)
                ins.then_inc(a_emb2, 1)
                for st in range(NSC):
                    for fc in range(32):
                        iu = st * 32 + fc
                        a.wait_ge(t_u, iu + 1)
                        a.wait_ge(d_mb1, 1)
                        if fc == 0 and st >= 1:
                            a.wait_ge(t_o, st * 4)
                        ins = a.activation(u_stC[:, fc * 256:fc * 256 + 256],
                                           ps_uC[:, (iu % 2) * 512:(iu % 2) * 512 + 256],
                                           AF.Gelu_apprx_tanh, bias=mb1c[:, fc:fc + 1])
                        ins.then_inc(a_eu, 1)

            @blk.tensor
            def _(t):
                t.wait_ge(a_g2, 1)
                t.wait_ge(qmisc, 64)
                for half in range(2):
                    t.matmul(ps_oC[0:1, half * 512:half * 512 + 512],
                             g8[:, :], tb2_sb[:, half * 512:half * 512 + 512],
                             start=True, stop=False)
                    ins = t.matmul(ps_oC[0:1, half * 512:half * 512 + 512],
                                   onesf_row[0:1, 0:1], b2r_sb[0:1, half * 512:half * 512 + 512],
                                   start=False, stop=True)
                    ins.then_inc(t_mb2, 1)
                for u in range(2 * NU):
                    v2 = u % 2
                    t.wait_ge(a_cst, u + 1)
                    if u >= 2:
                        t.wait_ge(a_emw, u - 1)
                    for b in range(4):
                        ins = t.matmul(ps_tC[:, v2 * 512 + b * 128:v2 * 512 + b * 128 + 128],
                                       accB2[:, v2 * 512 + b * 128:v2 * 512 + b * 128 + 128],
                                       idb[:, :], start=True, stop=True)
                    ins.then_inc(t_tpc, 1)
                t.wait_ge(qcw, 48)
                t.wait_ge(a_emb2, 1)
                for st in range(NSC):
                    t.wait_ge(qxnP[0], 16 * (st + 1))
                    for fc in range(32):
                        iu = st * 32 + fc
                        if iu >= 2:
                            t.wait_ge(a_eu, iu - 1)
                        for hc in range(8):
                            ins = t.matmul(ps_uC[:, (iu % 2) * 512:(iu % 2) * 512 + 256],
                                           mw1t[:, hc * 4096 + fc * 128:hc * 4096 + fc * 128 + 128],
                                           xn2C[:, hc * 256:hc * 256 + 256],
                                           start=(hc == 0), stop=(hc == 7))
                        ins.then_inc(t_u, 1)
                    t.wait_ge(a_eu, st * 32 + 32)
                    if st == 0:
                        t.wait_ge(qcw, 96)
                    for j in range(4):
                        sub, half = j // 2, j % 2
                        ou = st * 4 + j
                        if ou >= 2:
                            t.wait_ge(d_o, ou - 1)
                        for fc in range(32):
                            t.matmul(ps_oC[:, (ou % 2) * 512:(ou % 2) * 512 + 512],
                                     u_stC[:, fc * 256 + sub * 128:fc * 256 + sub * 128 + 128],
                                     mw2t[:, fc * 1024 + half * 512:fc * 1024 + half * 512 + 512],
                                     start=(fc == 0), stop=False)
                        ins = t.matmul(ps_oC[:, (ou % 2) * 512:(ou % 2) * 512 + 512],
                                       ones_row[0:1, 0:128],
                                       mb2row[0:1, half * 512:half * 512 + 512],
                                       start=False, stop=True)
                        ins.then_inc(t_o, 1)

        if debug:
            with nc.Block() as blk:
                @blk.sync
                def _(s):
                    s.wait_ge(qx2s, 16 * NST)
                    s.dma_start(out=dbg["x2"], in_=s_x2[:, :]).then_inc(qws, 16)
                    s.wait_ge(d_g, 1)
                    s.dma_start(out=dbg["g"], in_=g_row[:, :]).then_inc(qws, 16)
                    s.wait_ge(qws, 80 + 64)
    return nc


def _prep_core_inputs(inputs, T=4096):
    f32 = np.float32
    x = np.asarray(inputs["L0_attn_x"], dtype=f32)
    B, L, _ = x.shape
    cores = []
    com = {
        "wqT": np.ascontiguousarray(np.asarray(inputs["L0_attn_Wq"], f32).T),
        "wkT": np.ascontiguousarray(np.asarray(inputs["L0_attn_Wk"], f32).T),
        "wvT": np.ascontiguousarray(np.asarray(inputs["L0_attn_Wv"], f32).T),
        "woT": np.ascontiguousarray(np.asarray(inputs["L0_attn_Wo"], f32).T),
        "orfT": np.ascontiguousarray(np.asarray(inputs["L0_attn_orf"], f32).T),
        "bq": np.ascontiguousarray(np.asarray(inputs["L0_attn_bq"], f32).reshape(8, 128).T),
        "bk": np.ascontiguousarray(np.asarray(inputs["L0_attn_bk"], f32).reshape(8, 128).T),
        "bvfm": np.ascontiguousarray(np.asarray(inputs["L0_attn_bv"], f32).reshape(8, 128).T),
        "borow": np.ascontiguousarray(np.asarray(inputs["L0_attn_bo"], f32).reshape(1, D)),
        "gw1T": np.ascontiguousarray(np.asarray(inputs["L0_ffn_gate_w1"], f32).T),
        "gb1": np.ascontiguousarray(np.asarray(inputs["L0_ffn_gate_b1"], f32).reshape(8, 128).T),
        "gw2T": np.ascontiguousarray(np.asarray(inputs["L0_ffn_gate_w2"], f32).T),
        "gb2row": np.ascontiguousarray(np.asarray(inputs["L0_ffn_gate_b2"], f32).reshape(1, NE)),

        "b1c": np.ascontiguousarray(np.asarray(inputs["L0_ffn_b1"], f32).reshape(32, 128).T),
        "tb1c": np.ascontiguousarray(
            np.asarray(inputs["L0_ffn_Tb1"], f32).reshape(NE, 32, 128).transpose(2, 0, 1).reshape(128, NE * 32)),

        "b2row": np.ascontiguousarray(np.asarray(inputs["L0_ffn_b2"], f32).reshape(1, D)),
        "tb2": np.ascontiguousarray(np.asarray(inputs["L0_ffn_Tb2"], f32)),
        "ident": np.eye(128, dtype=f32),
    }
    W1 = np.asarray(inputs["L0_ffn_W1"], f32)
    TW1 = np.asarray(inputs["L0_ffn_TW1"], f32)
    W2 = np.asarray(inputs["L0_ffn_W2"], f32)
    TW2 = np.asarray(inputs["L0_ffn_TW2"], f32)
    halves = []
    for r in range(2):
        t1 = TW1[:, :, r * 512:(r + 1) * 512]
        t1 = np.ascontiguousarray(
            t1.reshape(NE, 32, 128, 512).transpose(1, 2, 0, 3).reshape(32, 128, NE * 512))
        t2 = TW2[:, :, r * 2048:(r + 1) * 2048]
        t2 = np.ascontiguousarray(
            t2.reshape(NE, 8, 128, 4, 512).transpose(1, 3, 2, 0, 4).reshape(8, 4, 128, NE * 512))
        halves.append({
            "w1h": np.ascontiguousarray(W1[:, r * 512:(r + 1) * 512]),
            "tw1h": t1,
            "w2h": np.ascontiguousarray(W2[:, r * 2048:(r + 1) * 2048]),
            "tw2h": t2,
        })
    for b in range(B):
        for h in range(2):
            im = dict(com)
            im.update(halves[h])
            im["x"] = np.ascontiguousarray(x[b, h * T:(h + 1) * T])
            cores.append(im)
    return cores


def kernel(**inputs):
    T = 4096
    nc = build(T=T, L_total=2 * T, debug=False)
    in_maps = _prep_core_inputs(inputs, T)
    res = run_bass_kernel_spmd(nc, in_maps, core_ids=list(range(8)))
    x = np.asarray(inputs["L0_attn_x"], dtype=np.float32)
    B, L, _ = x.shape
    out = np.empty((B, L, D), dtype=np.float32)
    for c in range(8):
        b, h = c // 2, c % 2
        out[b, h * T:(h + 1) * T] = res.results[c]["out"]
    return out
